# version 19
# baseline (speedup 1.0000x reference)
import numpy as np
import concourse.bass as bass
import concourse.mybir as mybir
from concourse.bass_utils import run_bass_kernel_spmd
from concourse.alu_op_type import AluOpType as Alu

F32 = mybir.dt.float32
BF16 = mybir.dt.bfloat16
I32 = mybir.dt.int32
AF = mybir.ActivationFunctionType
AX = mybir.AxisListType.X

D = 1024
DC = 8
F = 4096
INNER = 2048
NA = 16
FF = 4096
NO = 32
T = 512
SUB = 256
L = 4096
EPS = 1e-5


class _Q:
    __slots__ = ("name", "sem", "step", "count", "thunks")

    def __init__(self, name, sem, step, thunks):
        self.name, self.sem, self.step = name, sem, step
        self.count = 0
        self.thunks = thunks


class _Buf:
    __slots__ = ("w", "r")

    def __init__(self):
        self.w = None
        self.r = []


class _Ctx:
    def __init__(self):
        self.known = {}

    def op(self, q, ins, outs, fn, dq=None):
        deps = []
        if dq is not None and dq.count > 0:
            deps.append((dq, dq.count))
        for b in ins:
            if b.w is not None:
                deps.append(b.w)
        for b in outs:
            if b.w is not None:
                deps.append(b.w)
            deps.extend(b.r)
        need = {}
        for (tq, cnt) in deps:
            if need.get(tq.name, (None, 0))[1] < cnt:
                need[tq.name] = (tq, cnt)
        for (tq, cnt) in need.values():
            key = (q.name, tq.name)
            if self.known.get(key, 0) < cnt:
                sem = tq.sem
                q.thunks.append(lambda e, sem=sem, cnt=cnt: e.wait_ge(sem, cnt))
                self.known[key] = cnt
        tq = dq if dq is not None else q
        sem, step = tq.sem, tq.step
        q.thunks.append(lambda e, fn=fn, sem=sem, step=step:
                        fn(e).then_inc(sem, step))
        tq.count += tq.step
        c = (tq, tq.count)
        for b in ins:
            b.r.append(c)
        for b in outs:
            b.w = c
            b.r = []
        return c


def build_program(n_tiles=8):
    nc = bass.Bass("TRN2", target_bir_lowering=False)

    xT = nc.dram_tensor("xT", [D, L], F32, kind="ExternalInput")
    w_in = nc.dram_tensor("w_in", [D, F], F32, kind="ExternalInput")
    w_out = nc.dram_tensor("w_out", [INNER, D], F32, kind="ExternalInput")
    convw_d = nc.dram_tensor("convw", [128, 64], F32, kind="ExternalInput")
    convb_d = nc.dram_tensor("convb", [128, 16], F32, kind="ExternalInput")
    wgT = nc.dram_tensor("wgT", [D, FF], F32, kind="ExternalInput")
    wuT = nc.dram_tensor("wuT", [D, FF], F32, kind="ExternalInput")
    wdT = nc.dram_tensor("wdT", [FF, D], F32, kind="ExternalInput")
    outT = nc.dram_tensor("outT", [D, L], F32, kind="ExternalOutput")
    win_hi = nc.dram_tensor("win_hi", [D, F], BF16, kind="Internal")
    win_lo = nc.dram_tensor("win_lo", [D, F], BF16, kind="Internal")
    wout_hi = nc.dram_tensor("wout_hi", [INNER, D], BF16, kind="Internal")
    wout_lo = nc.dram_tensor("wout_lo", [INNER, D], BF16, kind="Internal")
    wgq = nc.dram_tensor("wgq", [D, FF], BF16, kind="Internal")
    wuq = nc.dram_tensor("wuq", [D, FF], BF16, kind="Internal")
    wdq = nc.dram_tensor("wdq", [FF, D], BF16, kind="Internal")

    A = nc.alloc_sbuf_tensor
    ones1 = A("ones1", [1, 128], F32)
    ones4 = A("ones4", [128, 4], F32)
    ident = A("ident", [128, 128], F32)
    iota_p = A("iota_p", [128, 128], F32)
    iota_f = A("iota_f", [128, 128], F32)
    convw = A("convw_s", [128, 64], F32)
    convb = A("convb_s", [128, 16], F32)
    carry = A("carry", [128, 48], F32)
    xbuf = A("xbuf", [128, DC * T], F32)
    xrbuf = A("xrbuf", [128, DC * T], F32)
    c2buf = A("c2buf", [128, DC * T], F32)
    sqsc = A("sqsc", [128, 2 * T], F32)
    xnsc = A("xnsc", [128, T], F32)
    xnh = A("xnh", [128, DC * T], BF16)
    xnl = A("xnl", [128, DC * T], BF16)
    glu = A("glu", [128, T + 3], F32)
    sigb = A("sigb", [128, T], F32)
    yact = A("yact", [128, 2 * T], F32)
    ybig = A("ybig", [128, 2 * NA * T], BF16)
    yh = ybig[:, 0:NA * T]
    yl = ybig[:, NA * T:2 * NA * T]
    ybig32 = ybig.bitcast(F32)
    p_buf = A("p_buf", [128, 16 * T], F32)
    qisc = A("qisc", [128, T], I32)
    qring = A("qring", [128, 2 * T], BF16)
    sg2 = A("sg2", [128, T], F32)
    sgs = A("sgs", [128, T], F32)
    dsc = A("dsc", [128, T], F32)
    m1sb = A("m1sb", [128, T], F32)
    s1sb = A("s1sb", [128, T], F32)
    a2sb = A("a2sb", [128, T], F32)
    rgsb = A("rgsb", [128, T], F32)
    bsb = a2sb
    rdsb = rgsb
    r_m = A("r_m", [1, T], F32)
    r_s = A("r_s", [1, T], F32)
    r_t0 = A("r_t0", [1, T], F32)
    r_t1 = A("r_t1", [1, T], F32)
    r_am = A("r_am", [1, T], F32)
    r_ru = A("r_ru", [1, T], F32)
    amax_tm = A("amax_tm", [128, 4], F32)
    a4 = A("a4", [4, 128], F32)
    r11a = A("r11a", [1, 1], F32)
    r11b = A("r11b", [1, 1], F32)
    r11c = A("r11c", [1, 1], F32)
    swp = A("swp", [128, 1], F32)
    wc_g = A("wc_g", [1, 1], F32)
    wc_u = A("wc_u", [1, 1], F32)
    wc_d = A("wc_d", [1, 1], F32)
    arena = A("arena", [128, 16384], BF16)
    arena32 = arena.bitcast(F32)
    arenaI = arena.bitcast(I32)

    WIH = [0, 2048]
    WIL = [4096, 6144]
    WOH = [8192, 10240]
    WOL = [12288, 14336]
    wtmpA = arena32[:, 0:2048]
    wtmpB = arena32[:, 2048:4096]
    hi_t = arena[:, 8192:10240]
    lo_t = arena[:, 10240:12288]
    p_bufI = p_buf.bitcast(I32)
    p_bufB = p_buf.bitcast(BF16)
    qtmpA = p_buf[:, 0:2048]
    qtmpB = p_buf[:, 2048:4096]
    abs_t = p_buf[:, 4096:6144]
    qi_t = p_bufI[:, 4096:6144]
    qb_t = p_bufB[:, 12288:14336]
    c2I = c2buf.bitcast(I32)
    xnhF = xnh.bitcast(F32)
    xnhI = xnh.bitcast(I32)
    xnlF = xnl.bitcast(F32)
    xnlI = xnl.bitcast(I32)

    pb = [nc.alloc_psum_tensor("pb%d" % i, [128, 512], F32) for i in range(8)]

    sT = nc.alloc_semaphore("sT")
    sV = nc.alloc_semaphore("sV")
    sS = nc.alloc_semaphore("sS")
    sG = nc.alloc_semaphore("sG")
    sdSY = [nc.alloc_semaphore("sdSY%d" % i) for i in range(6)]
    sdSC = [nc.alloc_semaphore("sdSC%d" % i) for i in range(2)]
    sdG = [nc.alloc_semaphore("sdG%d" % i) for i in range(4)]

    thT, thV, thS, thG, thSY = [], [], [], [], []
    qT = _Q("T", sT, 1, thT)
    qV = _Q("V", sV, 1, thV)
    qS = _Q("S", sS, 1, thS)
    qG = _Q("G", sG, 1, thG)
    qSY = _Q("SY", None, 1, thSY)
    dqSY = [_Q("dSY%d" % i, sdSY[i], 16, None) for i in range(6)]
    dqSC = [_Q("dSC%d" % i, sdSC[i], 16, None) for i in range(2)]
    dqG = [_Q("dG%d" % i, sdG[i], 16, None) for i in range(4)]
    rot = {"SY": [0], "SC": [0], "G": [0]}

    cx = _Ctx()
    B = _Buf
    b_x = B(); b_xr = B(); b_c2 = B()
    b_sq = [B(), B()]
    b_xnsc = B()
    b_xnh = B(); b_xnl = B()
    b_xq = [B() for _ in range(DC)]
    b_glu = B(); b_sig = B(); b_yact = [B(), B()]
    b_ybig = B()
    b_p = B(); b_qi = [B(), B()]; b_qr = [B(), B()]
    b_sg2 = B(); b_sgs = B(); b_dsc = B()
    b_m1 = B(); b_s1 = B(); b_a2 = B(); b_rg = B()
    b_bsb = b_a2; b_rd = b_rg
    b_rm = B(); b_rs = B(); b_rt0 = B(); b_rt1 = B()
    b_ram = B(); b_rru = B()
    b_atm = B(); b_a4 = B()
    b_r11a = B(); b_r11b = B(); b_r11c = B(); b_swp = B()
    b_wc = {"g": B(), "u": B(), "d": B()}
    wc_t = {"g": wc_g, "u": wc_u, "d": wc_d}
    b_carry = [B() for _ in range(NA)]
    b_consts = B()
    b_pb = [B() for _ in range(8)]
    b_wih = [B(), B()]; b_wil = [B(), B()]
    b_woh = [B(), B()]; b_wol = [B(), B()]
    b_wtmpA = B(); b_wtmpB = B(); b_hi = B(); b_lo = B()
    b_qit = B(); b_qbt = B()
    b_dwin = B(); b_dwout = B()
    b_dram_q = {"g": B(), "u": B(), "d": B()}
    b_dwd = [B() for _ in range(16)]
    b_stg_g = B(); b_stg_u = B(); b_stg_d = [B(), B()]
    b_stg4 = [B() for _ in range(4)]
    b_out = B()

    def dma(q, dqp, dst_ap, src_ap, ins, outs):
        if dqp is dqSY:
            key = "SY"
        elif dqp is dqSC:
            key = "SC"
        else:
            key = "G"
        i = rot[key][0]
        rot[key][0] = (i + 1) % len(dqp)
        return cx.op(q, ins, outs,
                     lambda e, d=dst_ap, s=src_ap: e.dma_start(out=d, in_=s),
                     dq=dqp[i])

    cx.op(qG, [], [b_consts], lambda e: e.memset(ones1[:], 1.0))
    cx.op(qG, [b_consts], [b_consts], lambda e: e.memset(ones4[:], 1.0))
    cx.op(qG, [b_consts], [b_consts], lambda e: e.memset(carry[:], 0.0))
    cx.op(qG, [b_consts], [b_consts], lambda e: e.memset(amax_tm[:], 0.0))
    cx.op(qG, [b_consts], [b_consts], lambda e: e.iota(
        iota_p[:], [[0, 128]], channel_multiplier=1,
        allow_small_or_imprecise_dtypes=True))
    cx.op(qG, [b_consts], [b_consts], lambda e: e.iota(
        iota_f[:], [[1, 128]], channel_multiplier=0,
        allow_small_or_imprecise_dtypes=True))
    cx.op(qV, [b_consts], [b_consts], lambda e: e.tensor_tensor(
        ident[:], iota_p[:], iota_f[:], Alu.is_equal))
    dma(qSY, dqSY, convw[:], convw_d.ap(), [b_consts], [b_consts])
    dma(qSY, dqSY, convb[:], convb_d.ap(), [b_consts], [b_consts])
    for i in range(NA):
        b_carry[i].w = b_consts.w

    def split_hilo(src_dram, dst_hi, dst_lo, rows, cols, b_dst):
        nch = rows // 128
        nh = cols // 2048 if cols >= 2048 else 1
        w = cols // nh
        n_it = nch * nh
        slots = [(wtmpA, b_wtmpA), (wtmpB, b_wtmpB)]

        def src_ap(i):
            c, h = divmod(i, nh)
            return bass.AP(src_dram, c * 128 * cols + h * w,
                           [[cols, 128], [1, w]])

        def dst_ap(dst, i):
            c, h = divmod(i, nh)
            return bass.AP(dst, c * 128 * cols + h * w, [[cols, 128], [1, w]])

        dma(qSY, dqSY, slots[0][0][:, 0:w], src_ap(0), [], [slots[0][1]])
        for i in range(n_it):
            wt, bwt = slots[i % 2]
            if i + 1 < n_it:
                wt2, bwt2 = slots[(i + 1) % 2]
                dma(qSY, dqSY, wt2[:, 0:w], src_ap(i + 1), [], [bwt2])
            cx.op(qS, [bwt], [b_hi], lambda e, wt=wt, w=w:
                  e.copy(hi_t[:, 0:w], wt[:, 0:w]))
            cx.op(qV, [bwt, b_hi], [b_lo], lambda e, wt=wt, w=w:
                  e.tensor_tensor(lo_t[:, 0:w], wt[:, 0:w],
                                  hi_t[:, 0:w], Alu.subtract))
            dma(qSY, dqSY, dst_ap(dst_hi, i), hi_t[:, 0:w], [b_hi], [b_dst])
            dma(qSY, dqSY, dst_ap(dst_lo, i), lo_t[:, 0:w], [b_lo], [b_dst])

    QW = {
        "g": (wgT, wgq, D, FF, 4, dqG),
        "u": (wuT, wuq, D, FF, 5, dqG),
        "d": (wdT, wdq, FF, D, 6, dqSC),
    }
    QQ = {"g": qG, "u": qG, "d": qS}
    swp3 = {k: A("swp_" + k, [128, 1], F32) for k in ("g", "u", "d")}
    b_swp3 = {k: B() for k in ("g", "u", "d")}

    def q_iters(key):
        rows_, cols_ = QW[key][2], QW[key][3]
        nch = rows_ // 128
        nh = max(1, cols_ // 2048)
        return nch * nh, cols_ // nh, nch, nh

    def q_src(key, i, dram_idx):
        dram_t = QW[key][dram_idx]
        n_it, w, nch, nh = q_iters(key)
        c, h = divmod(i, nh)
        return bass.AP(dram_t, c * 128 * QW[key][3] + h * w,
                       [[QW[key][3], 128], [1, w]])

    def quant_pass1(keys=("g", "u", "d")):
        its = {k: q_iters(k) for k in keys}
        seq = []
        for k in keys:
            for i in range(its[k][0]):
                seq.append((k, i))
        slots = [(qtmpA, b_wtmpA), (qtmpB, b_wtmpB)]
        started = {k: False for k in keys}
        k0, i0 = seq[0]
        dma(QQ[k0], QW[k0][5], slots[0][0][:, 0:its[k0][1]], q_src(k0, i0, 0),
            [], [slots[0][1]])
        for n, (k, i) in enumerate(seq):
            wt, bwt = slots[n % 2]
            n_it, w, nch, nh = its[k]
            if n + 1 < len(seq):
                k2, i2 = seq[n + 1]
                wt2, bwt2 = slots[(n + 1) % 2]
                dma(QQ[k2], QW[k2][5], wt2[:, 0:its[k2][1]], q_src(k2, i2, 0),
                    [], [bwt2])
            cx.op(qS, [bwt], [bwt], lambda e, wt=wt, w=w:
                  e.activation(wt[:, 0:w], wt[:, 0:w], AF.Abs))
            bank = QW[k][4]
            nsl = w // 512
            for sl in range(nsl):
                st = not started[k]
                started[k] = False if False else True
                last = (i == n_it - 1 and sl == nsl - 1)
                cx.op(qT, [bwt, b_consts], [b_pb[bank]],
                      lambda e, wt=wt, sl=sl, st=st, last=last, bank=bank:
                      e.matmul(pb[bank][0:1, 0:512], ones4[:, 0:1],
                               wt[:, sl * 512:(sl + 1) * 512],
                               start=st, stop=last))

    def quant_rows(key):
        bank = QW[key][4]
        nel = float(QW[key][2] * QW[key][3])
        cx.op(qV, [b_pb[bank]], [b_r11a], lambda e, bank=bank: e.tensor_reduce(
            r11a[:], pb[bank][0:1, 0:512], axis=AX, op=Alu.add))
        cx.op(qV, [b_r11a], [b_r11b], lambda e, nel=nel: e.tensor_scalar(
            r11b[:], r11a[:], 1.0 / nel, None, Alu.mult))
        cx.op(qV, [b_r11b], [b_r11a], lambda e: e.tensor_scalar(
            r11a[:], r11b[:], 1e-5, None, Alu.max))
        cx.op(qV, [b_r11a], [b_r11b], lambda e: e.reciprocal(r11b[:], r11a[:]))
        cx.op(qV, [b_r11b], [b_r11c], lambda e: e.reciprocal(r11c[:], r11b[:]))
        cx.op(qV, [b_r11c], [b_wc[key]], lambda e, t=wc_t[key]: e.tensor_scalar(
            t[:], r11c[:], 1.0 / 127.0, None, Alu.mult))
        cx.op(qT, [b_r11b, b_consts], [b_pb[7]], lambda e: e.matmul(
            pb[7][:, 0:1], ones1[:], r11b[:], start=True, stop=True))
        cx.op(qV, [b_pb[7]], [b_swp3[key]], lambda e, t=swp3[key]:
              e.tensor_copy(t[:], pb[7][:, 0:1]))

    def xs(c):
        return xbuf[:, c * T:(c + 1) * T]

    def xrs(c):
        return xrbuf[:, c * T:(c + 1) * T]

    def x_load(t):
        src = bass.AP(xT, t * T, [[L, 128], [L * 128, DC], [1, T]])
        dst = bass.AP(xbuf, 0, [[DC * T, 128], [T, DC], [1, T]])
        dma(qSY, dqSY, dst, src, [], [b_x])

    def out_store(t):
        dst = bass.AP(outT, t * T, [[L, 128], [L * 128, DC], [1, T]])
        src = bass.AP(c2buf, 0, [[DC * T, 128], [T, DC], [1, T]])
        dma(qSY, dqSY, dst, src, [b_c2], [b_out])

    def ln_stats(src_buf, src_tile):
        for c in range(DC):
            cx.op(qT, [src_buf, b_consts], [b_pb[4]],
                  lambda e, c=c: e.matmul(pb[4][0:4, :], ones4[:], src_tile(c),
                                          start=(c == 0), stop=(c == DC - 1)))
        for c in range(DC):
            sq = sqsc[:, (c % 2) * T:(c % 2) * T + T]
            cx.op(qV, [src_buf], [b_sq[c % 2]],
                  lambda e, c=c, sq=sq: e.tensor_tensor(
                      sq, src_tile(c), src_tile(c), Alu.mult))
            cx.op(qT, [b_sq[c % 2], b_consts], [b_pb[5]],
                  lambda e, c=c, sq=sq: e.matmul(
                      pb[5][0:4, :], ones4[:], sq,
                      start=(c == 0), stop=(c == DC - 1)))
        cx.op(qV, [b_pb[4]], [b_rm], lambda e: e.tensor_scalar(
            r_m[:], pb[4][0:1, :], 1.0 / D, None, Alu.mult))
        cx.op(qV, [b_pb[5]], [b_rt0], lambda e: e.tensor_scalar(
            r_t0[:], pb[5][0:1, :], 1.0 / D, None, Alu.mult))
        cx.op(qV, [b_rm], [b_rt1], lambda e: e.tensor_tensor(
            r_t1[:], r_m[:], r_m[:], Alu.mult))
        cx.op(qV, [b_rt0, b_rt1], [b_ram], lambda e: e.tensor_tensor(
            r_am[:], r_t0[:], r_t1[:], Alu.subtract))
        cx.op(qV, [b_ram], [b_rt1], lambda e: e.tensor_scalar(
            r_t1[:], r_am[:], EPS, None, Alu.add))
        cx.op(qS, [b_rt1], [b_rt0], lambda e: e.activation(
            r_t0[:], r_t1[:], AF.Sqrt))
        cx.op(qV, [b_rt0], [b_rs], lambda e: e.reciprocal(r_s[:], r_t0[:]))

    def bcast_row(row_h, row_buf, bank):
        def f(e):
            last = None
            for j in range(4):
                last = e.matmul(pb[bank][:, j * 128:(j + 1) * 128], ones1[:],
                                row_h[0:1, j * 128:(j + 1) * 128],
                                start=(j == 0), stop=(j == 3))
            return last
        cx.op(qT, [row_buf, b_consts], [b_pb[bank]], f)

    def amax_dance(mx_buf, mx_tile, width):
        nb = width // 128
        for j in range(nb):
            bk = 4 + (j % 2)
            cx.op(qT, [mx_buf, b_consts], [b_pb[bk]], lambda e, j=j, bk=bk:
                  e.transpose(pb[bk][:, 0:128],
                              mx_tile[:, j * 128:(j + 1) * 128], ident[:]))
            cx.op(qV, [b_pb[bk]], [b_atm], lambda e, j=j, bk=bk:
                  e.tensor_reduce(amax_tm[:, j:j + 1], pb[bk][:, 0:128],
                                  axis=AX, op=Alu.max,
                                  apply_absolute_value=True))
        cx.op(qT, [b_atm, b_consts], [b_pb[4]], lambda e: e.transpose(
            pb[4][0:4, 0:128], amax_tm[:], ident[:]))
        cx.op(qV, [b_pb[4]], [b_a4], lambda e: e.tensor_copy(
            a4[:], pb[4][0:4, 0:128]))
        dst = bass.AP(r_am, 0, [[T, 1], [1, width]])
        dma(qS, dqSC, dst, a4[0:nb, :], [b_a4], [b_ram])

    def wi_dma(g, q=None, dq=None):
        slot = g % 2
        for (dst_off, src_dram, bslab) in (
                (WIH[slot], win_hi, b_wih[slot]),
                (WIL[slot], win_lo, b_wil[slot])):
            for (blk, coloff) in ((0, g * 128), (1, INNER + g * 128)):
                src = bass.AP(src_dram, coloff,
                              [[F, 128], [F * 128, DC], [1, 128]])
                dst = bass.AP(arena, dst_off + blk * 128,
                              [[16384, 128], [256, DC], [1, 128]])
                dma(q or qSY, dq or dqSY, dst, src, [b_dwin], [bslab])

    def wo_dma(sl, q=None, dq=None):
        slot = sl % 2
        for (dst_off, src_dram, bslab) in (
                (WOH[slot], wout_hi, b_woh[slot]),
                (WOL[slot], wout_lo, b_wol[slot])):
            src = bass.AP(src_dram, sl * 2 * 128 * D,
                          [[D, 128], [D * 128, 2], [1, D]])
            dst = bass.AP(arena, dst_off, [[16384, 128], [1024, 2], [1, D]])
            dma(q or qSY, dq or dqSY, dst, src, [b_dwout], [bslab])

    def wi_split(g):
        slot = g % 2
        r = g % 4
        bwt = b_stg4[r]
        for (blk, coloff) in ((0, g * 128), (1, INNER + g * 128)):
            srcap = bass.AP(w_in, coloff, [[F, 128], [F * 128, DC], [1, 128]])
            dst = bass.AP(p_buf, r * 2048 + blk * 128,
                          [[8192, 128], [256, DC], [1, 128]])
            dma(qSY, dqSY, dst, srcap, [], [bwt])
        hi_ap = arena[:, WIH[slot]:WIH[slot] + 2048]
        lo_ap = arena[:, WIL[slot]:WIL[slot] + 2048]
        src_ap = p_buf[:, r * 2048:r * 2048 + 2048]
        cx.op(qS, [bwt], [b_wih[slot]], lambda e, hi_ap=hi_ap, src_ap=src_ap:
              e.copy(hi_ap, src_ap))
        cx.op(qV, [bwt, b_wih[slot]], [b_wil[slot]],
              lambda e, hi_ap=hi_ap, lo_ap=lo_ap, src_ap=src_ap:
              e.tensor_tensor(lo_ap, src_ap, hi_ap, Alu.subtract))
        for (blk, coloff) in ((0, g * 128), (1, INNER + g * 128)):
            dhi = bass.AP(win_hi, coloff, [[F, 128], [F * 128, DC], [1, 128]])
            dlo = bass.AP(win_lo, coloff, [[F, 128], [F * 128, DC], [1, 128]])
            shi = bass.AP(arena, WIH[slot] + blk * 128,
                          [[16384, 128], [256, DC], [1, 128]])
            slo = bass.AP(arena, WIL[slot] + blk * 128,
                          [[16384, 128], [256, DC], [1, 128]])
            dma(qS, dqSC, dhi, shi, [b_wih[slot]], [b_dwin])
            dma(qS, dqSC, dlo, slo, [b_wil[slot]], [b_dwin])

    def wo_split(sl):
        slot = sl % 2
        r = 2 + sl % 2
        bwt = b_stg4[r]
        srcap = bass.AP(w_out, sl * 2 * 128 * D, [[D, 128], [D * 128, 2], [1, D]])
        dst = bass.AP(p_buf, r * 2048, [[8192, 128], [1024, 2], [1, D]])
        dma(qSY, dqSY, dst, srcap, [], [bwt])
        hi_ap = arena[:, WOH[slot]:WOH[slot] + 2048]
        lo_ap = arena[:, WOL[slot]:WOL[slot] + 2048]
        src_ap = p_buf[:, r * 2048:r * 2048 + 2048]
        cx.op(qS, [bwt], [b_woh[slot]], lambda e, hi_ap=hi_ap, src_ap=src_ap:
              e.copy(hi_ap, src_ap))
        cx.op(qV, [bwt, b_woh[slot]], [b_wol[slot]],
              lambda e, hi_ap=hi_ap, lo_ap=lo_ap, src_ap=src_ap:
              e.tensor_tensor(lo_ap, src_ap, hi_ap, Alu.subtract))
        dhi = bass.AP(wout_hi, sl * 2 * 128 * D, [[D, 128], [D * 128, 2], [1, D]])
        dlo = bass.AP(wout_lo, sl * 2 * 128 * D, [[D, 128], [D * 128, 2], [1, D]])
        shi = bass.AP(arena, WOH[slot], [[16384, 128], [1024, 2], [1, D]])
        slo = bass.AP(arena, WOL[slot], [[16384, 128], [1024, 2], [1, D]])
        dma(qS, dqSC, dhi, shi, [b_woh[slot]], [b_dwout])
        dma(qS, dqSC, dlo, slo, [b_wol[slot]], [b_dwout])

    def wg_dma(m):
        slot = m % 2
        for (off, dram_t, bslab, bd) in (
                (WIH[slot], wgq, b_wih[slot], b_dram_q["g"]),
                (WIL[slot], wuq, b_wil[slot], b_dram_q["u"])):
            src = bass.AP(dram_t, 2 * m * 128,
                          [[FF, 128], [FF * 128, DC], [1, 256]])
            dst = bass.AP(arena, off, [[16384, 128], [256, DC], [1, 256]])
            dma(qG, dqG, dst, src, [bd], [bslab])

    def wg_quant_slab(m):
        for (key, src_dram, dst_dram, off, bslab, b_stg, stg_off) in (
                ("g", wgT, wgq, WIH[m % 2], b_wih[m % 2], b_stg_g, 0),
                ("u", wuT, wuq, WIL[m % 2], b_wil[m % 2], b_stg_u, 2048)):
            srcap = bass.AP(src_dram, 2 * m * 128,
                            [[FF, 128], [FF * 128, DC], [1, 256]])
            dst_in = bass.AP(c2buf, stg_off, [[4096, 128], [256, DC], [1, 256]])
            dma(qS, dqSC, dst_in, srcap, [], [b_stg])
            in_ap = c2buf[:, stg_off:stg_off + 2048]
            qi_ap = c2I[:, stg_off:stg_off + 2048]
            slot_ap = arena[:, off:off + 2048]
            cx.op(qS, [b_swp3[key]], [b_stg],
                  lambda e, in_ap=in_ap, qi_ap=qi_ap, key=key:
                  e.activation(qi_ap, in_ap, AF.Copy, scale=swp3[key][:]))
            cx.op(qV, [b_stg], [bslab],
                  lambda e, qi_ap=qi_ap, slot_ap=slot_ap:
                  e.tensor_scalar(slot_ap, qi_ap, -1.0, 1.0, Alu.max, Alu.min))
            dq_ap = bass.AP(dst_dram, 2 * m * 128,
                            [[FF, 128], [FF * 128, DC], [1, 256]])
            sl_ap = bass.AP(arena, off, [[16384, 128], [256, DC], [1, 256]])
            dma(qSY, dqSY, dq_ap, sl_ap, [bslab], [b_dram_q[key]])

    def wd_quant_slab(m):
        b_stg = b_stg_d[m % 2]
        stgF = xnlF if m % 2 == 0 else xnhF
        stgI = xnlI if m % 2 == 0 else xnhI
        slot = m % 2
        srcap = bass.AP(wdT, 2 * m * 128 * D, [[D, 128], [D * 128, 2], [1, D]])
        dst_in = bass.AP(stgF, 0, [[2048, 128], [1024, 2], [1, D]])
        dma(qS, dqSC, dst_in, srcap, [], [b_stg])
        slot_ap = arena[:, WOH[slot]:WOH[slot] + 2048]
        cx.op(qS, [b_swp3["d"]], [b_stg],
              lambda e, stgF=stgF, stgI=stgI:
              e.activation(stgI[:, 0:2048], stgF[:, 0:2048], AF.Copy,
                           scale=swp3["d"][:]))
        cx.op(qV, [b_stg], [b_woh[slot]],
              lambda e, stgI=stgI, slot_ap=slot_ap:
              e.tensor_scalar(slot_ap, stgI[:, 0:2048], -1.0, 1.0,
                              Alu.max, Alu.min))
        dq_ap = bass.AP(wdq, 2 * m * 128 * D, [[D, 128], [D * 128, 2], [1, D]])
        sl_ap = bass.AP(arena, WOH[slot], [[16384, 128], [1024, 2], [1, D]])
        dma(qSY, dqSY, dq_ap, sl_ap, [b_woh[slot]], [b_dram_q["d"], b_dwd[m]])

    def wd_dma(m):
        slot = m % 2
        src = bass.AP(wdq, 2 * m * 128 * D, [[D, 128], [D * 128, 2], [1, D]])
        dst = bass.AP(arena, WOH[slot], [[16384, 128], [1024, 2], [1, D]])
        dma(qG, dqG, dst, src, [b_dwd[m]], [b_woh[slot]])

    def p_ap(o):
        if o < 16:
            return p_buf[:, o * T:(o + 1) * T]
        return ybig32[:, (o - 16) * T:(o - 15) * T]

    def p_bufs(o):
        return b_p if o < 16 else b_ybig

    def ffn(t, pre_next=False):
        wg_fn = wg_quant_slab if t == 0 else wg_dma
        wd_fn = wd_quant_slab if t == 0 else wd_dma
        gu_units, dn_units = ([], [])
        if pre_next:
            gu_units, dn_units = pre_hook_units()
        if t == 0:
            for bb in (b_stg_g, b_stg_u):
                bb.r.append((qV, qV.count))
            b_stg_d[0].r.append((qT, qT.count))
        wg_fn(0)
        wg_fn(1)
        mxs = [sqsc[:, 0:T], sqsc[:, T:2 * T]]
        cur = 0
        for o in range(NO):
            slot = (o // 2) % 2
            gb, ub = 2 * (o % 2), 2 * (o % 2) + 1

            if o < 2:
                for c in range(DC):
                    def mm_gu1(e, o=o, slot=slot, gb=gb, ub=ub, c=c):
                        j = o % 2
                        lg = arena[:, WIH[slot] + c * 256 + j * 128:
                                   WIH[slot] + c * 256 + j * 128 + 128]
                        lu = arena[:, WIL[slot] + c * 256 + j * 128:
                                   WIL[slot] + c * 256 + j * 128 + 128]
                        rq = xnh[:, c * T:(c + 1) * T]
                        e.matmul(pb[gb][:], lg, rq,
                                 start=(c == 0), stop=(c == DC - 1))
                        return e.matmul(pb[ub][:], lu, rq,
                                        start=(c == 0), stop=(c == DC - 1))
                    cx.op(qT, [b_wih[slot], b_wil[slot], b_xq[c]],
                          [b_pb[gb], b_pb[ub]], mm_gu1)
            else:
                def mm_gu(e, o=o, slot=slot, gb=gb, ub=ub):
                    last = None
                    j = o % 2
                    for c in range(DC):
                        lg = arena[:, WIH[slot] + c * 256 + j * 128:
                                   WIH[slot] + c * 256 + j * 128 + 128]
                        lu = arena[:, WIL[slot] + c * 256 + j * 128:
                                   WIL[slot] + c * 256 + j * 128 + 128]
                        rq = xnh[:, c * T:(c + 1) * T]
                        e.matmul(pb[gb][:], lg, rq,
                                 start=(c == 0), stop=(c == DC - 1))
                        last = e.matmul(pb[ub][:], lu, rq,
                                        start=(c == 0), stop=(c == DC - 1))
                    return last
                cx.op(qT, [b_wih[slot], b_wil[slot]] + [b_xq[c] for c in range(DC)],
                      [b_pb[gb], b_pb[ub]], mm_gu)
            if o % 2 == 1 and o // 2 + 2 < NO // 2:
                wg_fn(o // 2 + 2)
            if gu_units and o % 2 == 0 and o >= 2:
                gu_units.pop(0)()
            cx.op(qV, [b_pb[gb], b_rg], [b_sg2],
                  lambda e, gb=gb: e.tensor_tensor(
                      sg2[:], pb[gb][:], rgsb[:], Alu.mult))
            cx.op(qS, [b_sg2], [b_sgs], lambda e:
                  e.activation(sgs[:], sg2[:], AF.Sigmoid))
            cx.op(qV, [b_sg2, b_sgs], [b_dsc], lambda e:
                  e.tensor_tensor(dsc[:], sg2[:], sgs[:], Alu.mult))
            cx.op(qV, [b_dsc, b_pb[ub]], [p_bufs(o)],
                  lambda e, o=o, ub=ub: e.tensor_tensor(
                      p_ap(o), dsc[:], pb[ub][:], Alu.mult))
            if o == 0:
                cx.op(qV, [p_bufs(o)], [b_sq[0]],
                      lambda e, o=o: e.scalar_tensor_tensor(
                          mxs[0], p_ap(o), -1.0, p_ap(o), Alu.mult, Alu.max))
                cur = 0
            else:
                cx.op(qV, [p_bufs(o), b_sq[cur]], [b_sq[1 - cur]],
                      lambda e, o=o, cur=cur: e.scalar_tensor_tensor(
                          mxs[1 - cur], p_ap(o), -1.0, mxs[cur],
                          Alu.mult, Alu.max))
                cur = 1 - cur
                cx.op(qV, [p_bufs(o), b_sq[cur]], [b_sq[1 - cur]],
                      lambda e, o=o, cur=cur: e.tensor_tensor(
                          mxs[1 - cur], p_ap(o), mxs[cur], Alu.max))
                cur = 1 - cur
        if t == 0:
            b_stg_d[1].r.append((qT, qT.count))
            wd_fn(0)
        amax_dance(b_sq[cur], mxs[cur], T)
        cx.op(qV, [b_ram, b_rru], [b_rt0], lambda e: e.tensor_tensor(
            r_t0[:], r_am[:], r_ru[:], Alu.mult))
        cx.op(qV, [b_rt0], [b_rt1], lambda e: e.tensor_scalar(
            r_t1[:], r_t0[:], 1e-5, None, Alu.max))
        cx.op(qV, [b_rt1], [b_rt0], lambda e: e.reciprocal(
            r_t0[:], r_t1[:]))
        cx.op(qV, [b_rt0], [b_ram], lambda e: e.tensor_scalar(
            r_am[:], r_t0[:], 127.0, None, Alu.mult))
        cx.op(qV, [b_ram, b_rru], [b_rt0], lambda e: e.tensor_tensor(
            r_t0[:], r_am[:], r_ru[:], Alu.mult))
        cx.op(qV, [b_rt1, b_wc["d"]], [b_ram], lambda e: e.tensor_scalar(
            r_am[:], r_t1[:], wc_d[:], None, Alu.mult))

        def f_bc(e):
            last = None
            for j in range(4):
                e.matmul(pb[6][:, j * 128:(j + 1) * 128], ones1[:],
                         r_t0[0:1, j * 128:(j + 1) * 128],
                         start=(j == 0), stop=(j == 3))
            for j in range(4):
                last = e.matmul(pb[7][:, j * 128:(j + 1) * 128],
                                ones1[:], r_am[0:1, j * 128:(j + 1) * 128],
                                start=(j == 0), stop=(j == 3))
            return last
        cx.op(qT, [b_rt0, b_ram, b_consts], [b_pb[6], b_pb[7]], f_bc)
        cx.op(qV, [b_pb[6]], [b_bsb], lambda e: e.tensor_copy(
            bsb[:], pb[6][:]))
        cx.op(qV, [b_pb[7]], [b_rd], lambda e: e.tensor_copy(
            rdsb[:], pb[7][:]))
        if pre_next:
            wi_dma(0)
            wi_dma(1)
        if t != 0:
            wd_fn(0)
        wd_fn(1)
        for o in range(NO):
            h = o % 2
            slot = (o // 2) % 2
            cx.op(qV, [p_bufs(o), b_bsb], [b_qi[0]], lambda e, o=o:
                  e.tensor_tensor(qisc[:], p_ap(o), bsb[:], Alu.mult))
            cx.op(qV, [b_qi[0]], [b_qr[h]], lambda e, h=h: e.tensor_scalar(
                qring[:, h * T:(h + 1) * T], qisc[:],
                -128.0, 127.0, Alu.max, Alu.min))

            def mm_d(e, o=o, h=h, slot=slot):
                last = None
                j = o % 2
                for d in range(8):
                    lw = arena[:, WOH[slot] + j * 1024 + d * 128:
                               WOH[slot] + j * 1024 + d * 128 + 128]
                    last = e.matmul(pb[d][:], lw,
                                    qring[:, h * T:(h + 1) * T],
                                    start=(o == 0), stop=(o == NO - 1))
                return last
            cx.op(qT, [b_woh[slot], b_qr[h]],
                  [b_pb[d] for d in range(8)], mm_d)
            if o % 2 == 1 and o // 2 + 2 < NO // 2:
                wd_fn(o // 2 + 2)
            if dn_units and o % 4 == 0:
                dn_units.pop(0)()
        if t == 0:
            b_c2.r.append((qS, qS.count))
            for bb in (b_xnh, b_xnl):
                bb.r.append((qS, qS.count))
                bb.r.append((qV, qV.count))
        for d in range(DC):
            cx.op(qV, [b_pb[d], b_rd], [b_dsc], lambda e, d=d:
                  e.tensor_tensor(dsc[:], pb[d][:], rdsb[:], Alu.mult))
            cx.op(qV, [b_dsc, b_xr], [b_c2],
                  lambda e, d=d: e.tensor_tensor(
                      c2buf[:, d * T:(d + 1) * T], dsc[:],
                      xrs(d), Alu.add))

    def tile_mixer_pre(t):
        ln_stats(b_x, xs)
        bcast_row(r_m, b_rm, 6)
        cx.op(qV, [b_pb[6]], [b_m1], lambda e: e.tensor_copy(m1sb[:], pb[6][:]))
        for c in range(DC):
            cx.op(qV, [b_x, b_m1], [b_c2], lambda e, c=c: e.tensor_tensor(
                c2buf[:, c * T:(c + 1) * T], xs(c), m1sb[:], Alu.subtract))
        bcast_row(r_s, b_rs, 7)
        cx.op(qV, [b_pb[7]], [b_s1], lambda e: e.tensor_copy(s1sb[:], pb[7][:]))
        xn2v = sqsc[:, 0:T]
        for c in range(DC):
            cx.op(qV, [b_c2, b_s1], [b_sq[0]], lambda e, c=c, xn2v=xn2v:
                  e.tensor_tensor(xn2v, c2buf[:, c * T:(c + 1) * T],
                                  s1sb[:], Alu.mult))
            cx.op(qV, [b_sq[0]], [b_xnh, b_xq[c]], lambda e, c=c, xn2v=xn2v:
                  e.tensor_copy(xnh[:, c * T:(c + 1) * T], xn2v))
            cx.op(qV, [b_sq[0], b_xnh], [b_xnl], lambda e, c=c, xn2v=xn2v:
                  e.tensor_tensor(xnl[:, c * T:(c + 1) * T], xn2v,
                                  xnh[:, c * T:(c + 1) * T], Alu.subtract))


    def pre_hook_units():
        gu = []

        def u_sums():
            def f(e):
                last = None
                for c in range(DC):
                    last = e.matmul(pb[4][0:4, :], ones4[:], xs(c),
                                    start=(c == 0), stop=(c == DC - 1))
                return last
            cx.op(qT, [b_x, b_consts], [b_pb[4]], f)
        gu.append(u_sums)

        def u_sumsq(c):
            def emit():
                cx.op(qV, [b_x], [b_xnsc], lambda e, c=c: e.tensor_tensor(
                    xnsc[:], xs(c), xs(c), Alu.mult))
                cx.op(qT, [b_xnsc, b_consts], [b_pb[5]],
                      lambda e, c=c: e.matmul(pb[5][0:4, :], ones4[:],
                                              xnsc[:], start=(c == 0),
                                              stop=(c == DC - 1)))
            return emit
        for c in range(DC):
            gu.append(u_sumsq(c))

        def u_rows():
            cx.op(qV, [b_pb[4]], [b_rm], lambda e: e.tensor_scalar(
                r_m[:], pb[4][0:1, :], 1.0 / D, None, Alu.mult))
            cx.op(qV, [b_pb[5]], [b_rt0], lambda e: e.tensor_scalar(
                r_t0[:], pb[5][0:1, :], 1.0 / D, None, Alu.mult))
            cx.op(qV, [b_rm], [b_rt1], lambda e: e.tensor_tensor(
                r_t1[:], r_m[:], r_m[:], Alu.mult))
            cx.op(qV, [b_rt0, b_rt1], [b_ram], lambda e: e.tensor_tensor(
                r_am[:], r_t0[:], r_t1[:], Alu.subtract))
            cx.op(qV, [b_ram], [b_rt1], lambda e: e.tensor_scalar(
                r_t1[:], r_am[:], EPS, None, Alu.add))
            cx.op(qS, [b_rt1], [b_rt0], lambda e: e.activation(
                r_t0[:], r_t1[:], AF.Sqrt))
            cx.op(qV, [b_rt0], [b_rs], lambda e: e.reciprocal(r_s[:], r_t0[:]))
        gu.append(u_rows)

        def u_bc_m():
            bcast_row(r_m, b_rm, 6)
            cx.op(qV, [b_pb[6]], [b_m1],
                  lambda e: e.tensor_copy(m1sb[:], pb[6][:]))
        gu.append(u_bc_m)

        def u_bc_s():
            bcast_row(r_s, b_rs, 7)
            cx.op(qV, [b_pb[7]], [b_s1],
                  lambda e: e.tensor_copy(s1sb[:], pb[7][:]))
        gu.append(u_bc_s)

        dn = []

        def u_xn(c):
            def emit():
                xn2v = sqsc[:, 0:T]
                cx.op(qV, [b_x, b_m1], [b_c2], lambda e, c=c: e.tensor_tensor(
                    c2buf[:, c * T:(c + 1) * T], xs(c), m1sb[:],
                    Alu.subtract))
                cx.op(qV, [b_c2, b_s1], [b_sq[0]], lambda e, c=c, xn2v=xn2v:
                      e.tensor_tensor(xn2v, c2buf[:, c * T:(c + 1) * T],
                                      s1sb[:], Alu.mult))
                cx.op(qV, [b_sq[0]], [b_xnh, b_xq[c]],
                      lambda e, c=c, xn2v=xn2v: e.tensor_copy(
                          xnh[:, c * T:(c + 1) * T], xn2v))
                cx.op(qV, [b_sq[0], b_xnh], [b_xnl],
                      lambda e, c=c, xn2v=xn2v: e.tensor_tensor(
                          xnl[:, c * T:(c + 1) * T], xn2v,
                          xnh[:, c * T:(c + 1) * T], Alu.subtract))
            return emit
        for c in range(DC):
            dn.append(u_xn(c))
        return gu, dn

    def tile_mixer_main(t, skip_slabs=False):
        if t == 0:
            wi_fn, wo_fn = wi_split, wo_split
        elif t == 1:
            wi_fn = lambda g: wi_dma(g, qS, dqSC)
            wo_fn = lambda s: wo_dma(s, qS, dqSC)
        else:
            wi_fn, wo_fn = wi_dma, wo_dma
        if not skip_slabs:
            wi_fn(0)
            wi_fn(1)
        for g in range(NA):
            slot = g % 2
            ba, bg = 2 * (g % 2), 2 * (g % 2) + 1

            def mm_u(e, slot=slot, ba=ba, bg=bg):
                last = None
                for c in range(DC):
                    for (bank, blk) in ((ba, 0), (bg, 1)):
                        lh = arena[:, WIH[slot] + c * 256 + blk * 128:
                                   WIH[slot] + c * 256 + blk * 128 + 128]
                        ll = arena[:, WIL[slot] + c * 256 + blk * 128:
                                   WIL[slot] + c * 256 + blk * 128 + 128]
                        rh = xnh[:, c * T:(c + 1) * T]
                        rl = xnl[:, c * T:(c + 1) * T]
                        e.matmul(pb[bank][:], lh, rh, start=(c == 0), stop=False)
                        e.matmul(pb[bank][:], lh, rl, start=False, stop=False)
                        last = e.matmul(pb[bank][:], ll, rh, start=False,
                                        stop=(c == DC - 1))
                return last
            cx.op(qT, [b_wih[slot], b_wil[slot], b_xnh, b_xnl],
                  [b_pb[ba], b_pb[bg]], mm_u)
            sg_half = sigb[:, 0:T]
            cx.op(qS, [b_pb[bg]], [b_sig],
                  lambda e, bg=bg, sg_half=sg_half:
                  e.activation(sg_half, pb[bg][:], AF.Sigmoid))
            cx.op(qV, [b_carry[g]], [b_glu], lambda e, g=g: e.tensor_copy(
                glu[:, 0:3], carry[:, 3 * g:3 * g + 3]))
            cx.op(qV, [b_pb[ba], b_sig, b_glu], [b_glu],
                  lambda e, ba=ba, sg_half=sg_half: e.tensor_tensor(
                      glu[:, 3:3 + T], pb[ba][:], sg_half, Alu.mult))
            cx.op(qV, [b_glu], [b_carry[g]], lambda e, g=g: e.tensor_copy(
                carry[:, 3 * g:3 * g + 3], glu[:, T:T + 3]))
            ya, yb = yact[:, 0:T], yact[:, T:2 * T]
            cx.op(qV, [b_glu, b_consts], [b_yact[0]], lambda e, g=g, ya=ya:
                  e.tensor_scalar(ya, glu[:, 0:T],
                                  convw[:, 4 * g:4 * g + 1],
                                  convb[:, g:g + 1], Alu.mult, Alu.add))
            cur_t, oth_t = yb, ya
            buf_c, buf_o = b_yact[1], b_yact[0]
            for k in range(1, 4):
                cx.op(qV, [b_glu, b_consts, buf_o], [buf_c],
                      lambda e, g=g, k=k, ct=cur_t, ot=oth_t:
                      e.scalar_tensor_tensor(
                          ct, glu[:, k:k + T],
                          convw[:, 4 * g + k:4 * g + k + 1],
                          ot, Alu.mult, Alu.add))
                cur_t, oth_t = oth_t, cur_t
                buf_c, buf_o = buf_o, buf_c
            cx.op(qS, [buf_o], [b_sig], lambda e, rt=oth_t:
                  e.activation(sigb[:, 0:T], rt, AF.Sigmoid))
            cx.op(qV, [buf_o, b_sig], [buf_c],
                  lambda e, rt=oth_t, ot2=cur_t:
                  e.tensor_tensor(ot2, rt, sigb[:, 0:T], Alu.mult))
            cx.op(qV, [buf_c], [b_ybig], lambda e, g=g, ot2=cur_t: e.tensor_copy(
                yh[:, g * T:(g + 1) * T], ot2))
            cx.op(qV, [buf_c, b_ybig], [b_ybig], lambda e, g=g, ot2=cur_t:
                  e.tensor_tensor(yl[:, g * T:(g + 1) * T], ot2,
                                  yh[:, g * T:(g + 1) * T], Alu.subtract))
            if g + 2 < NA:
                wi_fn(g + 2)
            elif g == NA - 2:
                wo_fn(0)
                wo_fn(1)
        if t == 0:
            for bb in (b_wtmpA, b_wtmpB):
                bb.r.append((qS, qS.count))
                bb.r.append((qV, qV.count))

        for f_ in range(NA):
            slot = (f_ // 2) % 2

            def mm_o(e, f_=f_, slot=slot):
                last = None
                j = f_ % 2
                for d in range(DC):
                    lh = arena[:, WOH[slot] + j * 1024 + d * 128:
                               WOH[slot] + j * 1024 + d * 128 + 128]
                    ll = arena[:, WOL[slot] + j * 1024 + d * 128:
                               WOL[slot] + j * 1024 + d * 128 + 128]
                    rh = yh[:, f_ * T:(f_ + 1) * T]
                    rl = yl[:, f_ * T:(f_ + 1) * T]
                    e.matmul(pb[d][:], lh, rh, start=(f_ == 0), stop=False)
                    e.matmul(pb[d][:], lh, rl, start=False, stop=False)
                    last = e.matmul(pb[d][:], ll, rh, start=False,
                                    stop=(f_ == NA - 1))
                return last
            cx.op(qT, [b_woh[slot], b_wol[slot], b_ybig],
                  [b_pb[d] for d in range(8)], mm_o)
            if f_ % 2 == 1 and f_ // 2 + 2 < 8:
                wo_fn(f_ // 2 + 2)
        for d in range(DC):
            cx.op(qV, [b_pb[d], b_x], [b_xr], lambda e, d=d: e.tensor_tensor(
                xrs(d), pb[d][:], xs(d), Alu.add))
        if t + 1 < n_tiles:
            x_load(t + 1)


    def tile_ffn(t, pre_next=False):
        ln_stats(b_xr, xrs)
        bcast_row(r_m, b_rm, 6)
        cx.op(qV, [b_pb[6]], [b_m1], lambda e: e.tensor_copy(m1sb[:], pb[6][:]))
        mxs = [sqsc[:, 0:T], sqsc[:, T:2 * T]]
        cur = 0
        for c in range(DC):
            cx.op(qV, [b_xr, b_m1], [b_c2], lambda e, c=c: e.tensor_tensor(
                c2buf[:, c * T:(c + 1) * T], xrs(c), m1sb[:], Alu.subtract))
            if c == 0:
                cx.op(qV, [b_c2], [b_sq[0]], lambda e: e.scalar_tensor_tensor(
                    mxs[0], c2buf[:, 0:T], -1.0, c2buf[:, 0:T],
                    Alu.mult, Alu.max))
            else:
                cx.op(qV, [b_c2, b_sq[cur]], [b_sq[1 - cur]],
                      lambda e, c=c, cur=cur: e.scalar_tensor_tensor(
                          mxs[1 - cur], c2buf[:, c * T:(c + 1) * T], -1.0,
                          mxs[cur], Alu.mult, Alu.max))
                cur = 1 - cur
                cx.op(qV, [b_c2, b_sq[cur]], [b_sq[1 - cur]],
                      lambda e, c=c, cur=cur: e.tensor_tensor(
                          mxs[1 - cur], c2buf[:, c * T:(c + 1) * T],
                          mxs[cur], Alu.max))
                cur = 1 - cur
        amax_dance(b_sq[cur], mxs[cur], T)
        cx.op(qV, [b_ram, b_rs], [b_rt0], lambda e: e.tensor_tensor(
            r_t0[:], r_am[:], r_s[:], Alu.mult))
        cx.op(qV, [b_rt0], [b_rt1], lambda e: e.tensor_scalar(
            r_t1[:], r_t0[:], 1e-5, None, Alu.max))
        cx.op(qV, [b_rt1], [b_rt0], lambda e: e.reciprocal(r_t0[:], r_t1[:]))
        cx.op(qV, [b_rt0], [b_ram], lambda e: e.tensor_scalar(
            r_am[:], r_t0[:], 127.0, None, Alu.mult))
        cx.op(qV, [b_ram, b_rs], [b_rt0], lambda e: e.tensor_tensor(
            r_t0[:], r_am[:], r_s[:], Alu.mult))
        bcast_row(r_t0, b_rt0, 7)
        cx.op(qV, [b_pb[7]], [b_a2], lambda e: e.tensor_copy(a2sb[:], pb[7][:]))
        cx.op(qV, [b_rt1, b_wc["g"]], [b_ram], lambda e: e.tensor_scalar(
            r_am[:], r_t1[:], wc_g[:], None, Alu.mult))
        cx.op(qV, [b_rt1, b_wc["u"]], [b_rru], lambda e: e.tensor_scalar(
            r_ru[:], r_t1[:], wc_u[:], None, Alu.mult))
        bcast_row(r_am, b_ram, 6)
        cx.op(qV, [b_pb[6]], [b_rg], lambda e: e.tensor_copy(rgsb[:], pb[6][:]))
        for c in range(DC):
            cx.op(qV, [b_c2, b_a2], [b_qi[0]], lambda e, c=c:
                  e.tensor_tensor(qisc[:], c2buf[:, c * T:(c + 1) * T],
                                  a2sb[:], Alu.mult))
            cx.op(qV, [b_qi[0], b_xnh], [b_xq[c]], lambda e, c=c:
                  e.tensor_scalar(xnh[:, c * T:(c + 1) * T], qisc[:],
                                  -128.0, 127.0, Alu.max, Alu.min))

        ffn(t, pre_next)

        out_store(t)

    x_load(0)
    tile_mixer_pre(0)

    for t in range(n_tiles):
        if t == 0:
            tile_mixer_main(0)
            quant_pass1(("g",))
            quant_rows("g")
            quant_pass1(("u",))
            quant_rows("u")
            quant_pass1(("d",))
            quant_rows("d")
            f2 = [(qV, qV.count), (qT, qT.count), (qS, qS.count)]
            for sq_ in list(dqG) + list(dqSC):
                if sq_.count:
                    f2.append((sq_, sq_.count))
            b_p.r.extend(f2)
            tile_ffn(0)
        else:
            if t == 1:
                tile_mixer_pre(t)
            tile_mixer_main(t, skip_slabs=(t >= 2))
            tile_ffn(t, pre_next=(t < n_tiles - 1))

    for i, sq_ in enumerate(dqSY):
        if sq_.count:
            thSY.append(lambda e, s=sq_.sem, c=sq_.count: e.wait_ge(s, c))
    for i, sq_ in enumerate(dqG):
        if sq_.count:
            thG.append(lambda e, s=sq_.sem, c=sq_.count: e.wait_ge(s, c))
    for i, sq_ in enumerate(dqSC):
        if sq_.count:
            thS.append(lambda e, s=sq_.sem, c=sq_.count: e.wait_ge(s, c))

    with nc.Block() as block:
        @block.tensor
        def _(eng):
            for th in thT:
                th(eng)

        @block.vector
        def _(eng):
            for th in thV:
                th(eng)

        @block.scalar
        def _(eng):
            for th in thS:
                th(eng)

        @block.gpsimd
        def _(eng):
            for th in thG:
                th(eng)

        @block.sync
        def _(eng):
            for th in thSY:
                th(eng)

    return nc


def kernel(L0_attn_x, L0_attn_W_in, L0_attn_conv_w, L0_attn_conv_b,
           L0_attn_W_out, L0_ffn_w_gate, L0_ffn_w_up, L0_ffn_w_down):
    x = np.asarray(L0_attn_x)
    W_in = np.ascontiguousarray(np.asarray(L0_attn_W_in))
    W_out = np.ascontiguousarray(np.asarray(L0_attn_W_out))
    conv_w = np.asarray(L0_attn_conv_w)
    conv_b = np.asarray(L0_attn_conv_b)
    wgT_np = np.ascontiguousarray(np.asarray(L0_ffn_w_gate).T)
    wuT_np = np.ascontiguousarray(np.asarray(L0_ffn_w_up).T)
    wdT_np = np.ascontiguousarray(np.asarray(L0_ffn_w_down).T)
    convw_np = np.ascontiguousarray(
        conv_w.reshape(16, 128, 4).transpose(1, 0, 2).reshape(128, 64))
    convb_np = np.ascontiguousarray(conv_b.reshape(16, 128).T)

    nc = build_program(n_tiles=8)
    in_maps = []
    for b in range(8):
        xT_b = np.ascontiguousarray(x[b].T)
        in_maps.append({
            "xT": xT_b, "w_in": W_in, "w_out": W_out,
            "convw": convw_np, "convb": convb_np,
            "wgT": wgT_np, "wuT": wuT_np, "wdT": wdT_np,
        })
    res = run_bass_kernel_spmd(nc, in_maps, core_ids=list(range(8)))
    out = np.stack([np.asarray(res.results[b]["outT"]).T for b in range(8)],
                   axis=0)
    return np.ascontiguousarray(out, dtype=np.float32)


# version 20 (final)
# speedup vs baseline: 1.0001x; 1.0001x over previous
import numpy as np
import concourse.bass as bass
import concourse.mybir as mybir
from concourse.bass_utils import run_bass_kernel_spmd
from concourse.alu_op_type import AluOpType as Alu

F32 = mybir.dt.float32
BF16 = mybir.dt.bfloat16
I32 = mybir.dt.int32
AF = mybir.ActivationFunctionType
AX = mybir.AxisListType.X

D = 1024
DC = 8
F = 4096
INNER = 2048
NA = 16
FF = 4096
NO = 32
T = 512
SUB = 256
L = 4096
EPS = 1e-5


class _Q:
    __slots__ = ("name", "sem", "step", "count", "thunks")

    def __init__(self, name, sem, step, thunks):
        self.name, self.sem, self.step = name, sem, step
        self.count = 0
        self.thunks = thunks


class _Buf:
    __slots__ = ("w", "r")

    def __init__(self):
        self.w = None
        self.r = []


class _Ctx:
    def __init__(self):
        self.known = {}

    def op(self, q, ins, outs, fn, dq=None):
        deps = []
        if dq is not None and dq.count > 0:
            deps.append((dq, dq.count))
        for b in ins:
            if b.w is not None:
                deps.append(b.w)
        for b in outs:
            if b.w is not None:
                deps.append(b.w)
            deps.extend(b.r)
        need = {}
        for (tq, cnt) in deps:
            if need.get(tq.name, (None, 0))[1] < cnt:
                need[tq.name] = (tq, cnt)
        for (tq, cnt) in need.values():
            key = (q.name, tq.name)
            if self.known.get(key, 0) < cnt:
                sem = tq.sem
                q.thunks.append(lambda e, sem=sem, cnt=cnt: e.wait_ge(sem, cnt))
                self.known[key] = cnt
        tq = dq if dq is not None else q
        sem, step = tq.sem, tq.step
        q.thunks.append(lambda e, fn=fn, sem=sem, step=step:
                        fn(e).then_inc(sem, step))
        tq.count += tq.step
        c = (tq, tq.count)
        for b in ins:
            b.r.append(c)
        for b in outs:
            b.w = c
            b.r = []
        return c


def build_program(n_tiles=8):
    nc = bass.Bass("TRN2", target_bir_lowering=False)

    xT = nc.dram_tensor("xT", [D, L], F32, kind="ExternalInput")
    w_in = nc.dram_tensor("w_in", [D, F], F32, kind="ExternalInput")
    w_out = nc.dram_tensor("w_out", [INNER, D], F32, kind="ExternalInput")
    convw_d = nc.dram_tensor("convw", [128, 64], F32, kind="ExternalInput")
    convb_d = nc.dram_tensor("convb", [128, 16], F32, kind="ExternalInput")
    wgT = nc.dram_tensor("wgT", [D, FF], F32, kind="ExternalInput")
    wuT = nc.dram_tensor("wuT", [D, FF], F32, kind="ExternalInput")
    wdT = nc.dram_tensor("wdT", [FF, D], F32, kind="ExternalInput")
    outT = nc.dram_tensor("outT", [D, L], F32, kind="ExternalOutput")
    win_hi = nc.dram_tensor("win_hi", [D, F], BF16, kind="Internal")
    win_lo = nc.dram_tensor("win_lo", [D, F], BF16, kind="Internal")
    wout_hi = nc.dram_tensor("wout_hi", [INNER, D], BF16, kind="Internal")
    wout_lo = nc.dram_tensor("wout_lo", [INNER, D], BF16, kind="Internal")
    wgq = nc.dram_tensor("wgq", [D, FF], BF16, kind="Internal")
    wuq = nc.dram_tensor("wuq", [D, FF], BF16, kind="Internal")
    wdq = nc.dram_tensor("wdq", [FF, D], BF16, kind="Internal")

    A = nc.alloc_sbuf_tensor
    ones1 = A("ones1", [1, 128], F32)
    ones4 = A("ones4", [128, 4], F32)
    ident = A("ident", [128, 128], F32)
    iota_p = A("iota_p", [128, 128], F32)
    iota_f = A("iota_f", [128, 128], F32)
    convw = A("convw_s", [128, 64], F32)
    convb = A("convb_s", [128, 16], F32)
    carry = A("carry", [128, 48], F32)
    xbuf = A("xbuf", [128, DC * T], F32)
    xrbuf = A("xrbuf", [128, DC * T], F32)
    c2buf = A("c2buf", [128, DC * T], F32)
    sqsc = A("sqsc", [128, 2 * T], F32)
    xnsc = A("xnsc", [128, T], F32)
    xnh = A("xnh", [128, DC * T], BF16)
    xnl = A("xnl", [128, DC * T], BF16)
    glu = A("glu", [128, T + 3], F32)
    sigb = A("sigb", [128, T], F32)
    yact = A("yact", [128, 2 * T], F32)
    ybig = A("ybig", [128, 2 * NA * T], BF16)
    yh = ybig[:, 0:NA * T]
    yl = ybig[:, NA * T:2 * NA * T]
    ybig32 = ybig.bitcast(F32)
    p_buf = A("p_buf", [128, 16 * T], F32)
    qisc = A("qisc", [128, T], I32)
    qring = A("qring", [128, 2 * T], BF16)
    sg2 = A("sg2", [128, T], F32)
    sgs = A("sgs", [128, T], F32)
    dsc = A("dsc", [128, T], F32)
    m1sb = A("m1sb", [128, T], F32)
    s1sb = A("s1sb", [128, T], F32)
    a2sb = A("a2sb", [128, T], F32)
    rgsb = A("rgsb", [128, T], F32)
    bsb = a2sb
    rdsb = rgsb
    r_m = A("r_m", [1, T], F32)
    r_s = A("r_s", [1, T], F32)
    r_t0 = A("r_t0", [1, T], F32)
    r_t1 = A("r_t1", [1, T], F32)
    r_am = A("r_am", [1, T], F32)
    r_ru = A("r_ru", [1, T], F32)
    amax_tm = A("amax_tm", [128, 4], F32)
    a4 = A("a4", [4, 128], F32)
    r11a = A("r11a", [1, 1], F32)
    r11b = A("r11b", [1, 1], F32)
    r11c = A("r11c", [1, 1], F32)
    swp = A("swp", [128, 1], F32)
    wc_g = A("wc_g", [1, 1], F32)
    wc_u = A("wc_u", [1, 1], F32)
    wc_d = A("wc_d", [1, 1], F32)
    arena = A("arena", [128, 16384], BF16)
    arena32 = arena.bitcast(F32)
    arenaI = arena.bitcast(I32)

    WIH = [0, 2048]
    WIL = [4096, 6144]
    WOH = [8192, 10240]
    WOL = [12288, 14336]
    wtmpA = arena32[:, 0:2048]
    wtmpB = arena32[:, 2048:4096]
    hi_t = arena[:, 8192:10240]
    lo_t = arena[:, 10240:12288]
    p_bufI = p_buf.bitcast(I32)
    p_bufB = p_buf.bitcast(BF16)
    qtmpA = p_buf[:, 0:2048]
    qtmpB = p_buf[:, 2048:4096]
    abs_t = p_buf[:, 4096:6144]
    qi_t = p_bufI[:, 4096:6144]
    qb_t = p_bufB[:, 12288:14336]
    c2I = c2buf.bitcast(I32)
    xnhF = xnh.bitcast(F32)
    xnhI = xnh.bitcast(I32)
    xnlF = xnl.bitcast(F32)
    xnlI = xnl.bitcast(I32)

    pb = [nc.alloc_psum_tensor("pb%d" % i, [128, 512], F32) for i in range(8)]

    sT = nc.alloc_semaphore("sT")
    sV = nc.alloc_semaphore("sV")
    sS = nc.alloc_semaphore("sS")
    sG = nc.alloc_semaphore("sG")
    sdSY = [nc.alloc_semaphore("sdSY%d" % i) for i in range(6)]
    sdSC = [nc.alloc_semaphore("sdSC%d" % i) for i in range(2)]
    sdG = [nc.alloc_semaphore("sdG%d" % i) for i in range(4)]

    thT, thV, thS, thG, thSY = [], [], [], [], []
    qT = _Q("T", sT, 1, thT)
    qV = _Q("V", sV, 1, thV)
    qS = _Q("S", sS, 1, thS)
    qG = _Q("G", sG, 1, thG)
    qSY = _Q("SY", None, 1, thSY)
    dqSY = [_Q("dSY%d" % i, sdSY[i], 16, None) for i in range(6)]
    dqSC = [_Q("dSC%d" % i, sdSC[i], 16, None) for i in range(2)]
    dqG = [_Q("dG%d" % i, sdG[i], 16, None) for i in range(4)]
    rot = {"SY": [0], "SC": [0], "G": [0]}

    cx = _Ctx()
    B = _Buf
    b_x = B(); b_xr = B(); b_c2 = B()
    b_sq = [B(), B()]
    b_xnsc = B()
    b_xnh = B(); b_xnl = B()
    b_xq = [B() for _ in range(DC)]
    b_glu = B(); b_sig = B(); b_yact = [B(), B()]
    b_ybig = B()
    b_p = B(); b_qi = [B(), B()]; b_qr = [B(), B()]
    b_sg2 = B(); b_sgs = B(); b_dsc = B()
    b_m1 = B(); b_s1 = B(); b_a2 = B(); b_rg = B()
    b_bsb = b_a2; b_rd = b_rg
    b_rm = B(); b_rs = B(); b_rt0 = B(); b_rt1 = B()
    b_ram = B(); b_rru = B()
    b_atm = B(); b_a4 = B()
    b_r11a = B(); b_r11b = B(); b_r11c = B(); b_swp = B()
    b_wc = {"g": B(), "u": B(), "d": B()}
    wc_t = {"g": wc_g, "u": wc_u, "d": wc_d}
    b_carry = [B() for _ in range(NA)]
    b_consts = B()
    b_pb = [B() for _ in range(8)]
    b_wih = [B(), B()]; b_wil = [B(), B()]
    b_woh = [B(), B()]; b_wol = [B(), B()]
    b_wtmpA = B(); b_wtmpB = B(); b_hi = B(); b_lo = B()
    b_qit = B(); b_qbt = B()
    b_dwin = B(); b_dwout = B()
    b_dram_q = {"g": B(), "u": B(), "d": B()}
    b_dwd = [B() for _ in range(16)]
    b_stg_g = B(); b_stg_u = B(); b_stg_d = [B(), B()]
    b_stg4 = [B() for _ in range(4)]
    b_out = B()

    def dma(q, dqp, dst_ap, src_ap, ins, outs):
        if dqp is dqSY:
            key = "SY"
        elif dqp is dqSC:
            key = "SC"
        else:
            key = "G"
        i = rot[key][0]
        rot[key][0] = (i + 1) % len(dqp)
        return cx.op(q, ins, outs,
                     lambda e, d=dst_ap, s=src_ap: e.dma_start(out=d, in_=s),
                     dq=dqp[i])

    cx.op(qG, [], [b_consts], lambda e: e.memset(ones1[:], 1.0))
    cx.op(qG, [b_consts], [b_consts], lambda e: e.memset(ones4[:], 1.0))
    cx.op(qG, [b_consts], [b_consts], lambda e: e.memset(carry[:], 0.0))
    cx.op(qG, [b_consts], [b_consts], lambda e: e.memset(amax_tm[:], 0.0))
    cx.op(qG, [b_consts], [b_consts], lambda e: e.iota(
        iota_p[:], [[0, 128]], channel_multiplier=1,
        allow_small_or_imprecise_dtypes=True))
    cx.op(qG, [b_consts], [b_consts], lambda e: e.iota(
        iota_f[:], [[1, 128]], channel_multiplier=0,
        allow_small_or_imprecise_dtypes=True))
    cx.op(qV, [b_consts], [b_consts], lambda e: e.tensor_tensor(
        ident[:], iota_p[:], iota_f[:], Alu.is_equal))
    dma(qSY, dqSY, convw[:], convw_d.ap(), [b_consts], [b_consts])
    dma(qSY, dqSY, convb[:], convb_d.ap(), [b_consts], [b_consts])
    for i in range(NA):
        b_carry[i].w = b_consts.w

    def split_hilo(src_dram, dst_hi, dst_lo, rows, cols, b_dst):
        nch = rows // 128
        nh = cols // 2048 if cols >= 2048 else 1
        w = cols // nh
        n_it = nch * nh
        slots = [(wtmpA, b_wtmpA), (wtmpB, b_wtmpB)]

        def src_ap(i):
            c, h = divmod(i, nh)
            return bass.AP(src_dram, c * 128 * cols + h * w,
                           [[cols, 128], [1, w]])

        def dst_ap(dst, i):
            c, h = divmod(i, nh)
            return bass.AP(dst, c * 128 * cols + h * w, [[cols, 128], [1, w]])

        dma(qSY, dqSY, slots[0][0][:, 0:w], src_ap(0), [], [slots[0][1]])
        for i in range(n_it):
            wt, bwt = slots[i % 2]
            if i + 1 < n_it:
                wt2, bwt2 = slots[(i + 1) % 2]
                dma(qSY, dqSY, wt2[:, 0:w], src_ap(i + 1), [], [bwt2])
            cx.op(qS, [bwt], [b_hi], lambda e, wt=wt, w=w:
                  e.copy(hi_t[:, 0:w], wt[:, 0:w]))
            cx.op(qV, [bwt, b_hi], [b_lo], lambda e, wt=wt, w=w:
                  e.tensor_tensor(lo_t[:, 0:w], wt[:, 0:w],
                                  hi_t[:, 0:w], Alu.subtract))
            dma(qSY, dqSY, dst_ap(dst_hi, i), hi_t[:, 0:w], [b_hi], [b_dst])
            dma(qSY, dqSY, dst_ap(dst_lo, i), lo_t[:, 0:w], [b_lo], [b_dst])

    QW = {
        "g": (wgT, wgq, D, FF, 4, dqG),
        "u": (wuT, wuq, D, FF, 5, dqG),
        "d": (wdT, wdq, FF, D, 6, dqSC),
    }
    QQ = {"g": qG, "u": qG, "d": qS}
    swp3 = {k: A("swp_" + k, [128, 1], F32) for k in ("g", "u", "d")}
    b_swp3 = {k: B() for k in ("g", "u", "d")}

    def q_iters(key):
        rows_, cols_ = QW[key][2], QW[key][3]
        nch = rows_ // 128
        nh = max(1, cols_ // 2048)
        return nch * nh, cols_ // nh, nch, nh

    def q_src(key, i, dram_idx):
        dram_t = QW[key][dram_idx]
        n_it, w, nch, nh = q_iters(key)
        c, h = divmod(i, nh)
        return bass.AP(dram_t, c * 128 * QW[key][3] + h * w,
                       [[QW[key][3], 128], [1, w]])

    def quant_pass1(keys=("g", "u", "d")):
        its = {k: q_iters(k) for k in keys}
        seq = []
        for k in keys:
            for i in range(its[k][0]):
                seq.append((k, i))
        slots = [(qtmpA, b_wtmpA), (qtmpB, b_wtmpB)]
        started = {k: False for k in keys}
        k0, i0 = seq[0]
        dma(QQ[k0], QW[k0][5], slots[0][0][:, 0:its[k0][1]], q_src(k0, i0, 0),
            [], [slots[0][1]])
        for n, (k, i) in enumerate(seq):
            wt, bwt = slots[n % 2]
            n_it, w, nch, nh = its[k]
            if n + 1 < len(seq):
                k2, i2 = seq[n + 1]
                wt2, bwt2 = slots[(n + 1) % 2]
                dma(QQ[k2], QW[k2][5], wt2[:, 0:its[k2][1]], q_src(k2, i2, 0),
                    [], [bwt2])
            cx.op(qS, [bwt], [bwt], lambda e, wt=wt, w=w:
                  e.activation(wt[:, 0:w], wt[:, 0:w], AF.Abs))
            bank = QW[k][4]
            nsl = w // 512
            for sl in range(nsl):
                st = not started[k]
                started[k] = False if False else True
                last = (i == n_it - 1 and sl == nsl - 1)
                cx.op(qT, [bwt, b_consts], [b_pb[bank]],
                      lambda e, wt=wt, sl=sl, st=st, last=last, bank=bank:
                      e.matmul(pb[bank][0:1, 0:512], ones4[:, 0:1],
                               wt[:, sl * 512:(sl + 1) * 512],
                               start=st, stop=last))

    def quant_rows(key):
        bank = QW[key][4]
        nel = float(QW[key][2] * QW[key][3])
        cx.op(qV, [b_pb[bank]], [b_r11a], lambda e, bank=bank: e.tensor_reduce(
            r11a[:], pb[bank][0:1, 0:512], axis=AX, op=Alu.add))
        cx.op(qV, [b_r11a], [b_r11b], lambda e, nel=nel: e.tensor_scalar(
            r11b[:], r11a[:], 1.0 / nel, None, Alu.mult))
        cx.op(qV, [b_r11b], [b_r11a], lambda e: e.tensor_scalar(
            r11a[:], r11b[:], 1e-5, None, Alu.max))
        cx.op(qV, [b_r11a], [b_r11b], lambda e: e.reciprocal(r11b[:], r11a[:]))
        cx.op(qV, [b_r11b], [b_r11c], lambda e: e.reciprocal(r11c[:], r11b[:]))
        cx.op(qV, [b_r11c], [b_wc[key]], lambda e, t=wc_t[key]: e.tensor_scalar(
            t[:], r11c[:], 1.0 / 127.0, None, Alu.mult))
        cx.op(qT, [b_r11b, b_consts], [b_pb[7]], lambda e: e.matmul(
            pb[7][:, 0:1], ones1[:], r11b[:], start=True, stop=True))
        cx.op(qV, [b_pb[7]], [b_swp3[key]], lambda e, t=swp3[key]:
              e.tensor_copy(t[:], pb[7][:, 0:1]))

    def xs(c):
        return xbuf[:, c * T:(c + 1) * T]

    def xrs(c):
        return xrbuf[:, c * T:(c + 1) * T]

    def x_load(t):
        src = bass.AP(xT, t * T, [[L, 128], [L * 128, DC], [1, T]])
        dst = bass.AP(xbuf, 0, [[DC * T, 128], [T, DC], [1, T]])
        dma(qSY, dqSY, dst, src, [], [b_x])

    def out_store(t):
        dst = bass.AP(outT, t * T, [[L, 128], [L * 128, DC], [1, T]])
        src = bass.AP(c2buf, 0, [[DC * T, 128], [T, DC], [1, T]])
        dma(qSY, dqSY, dst, src, [b_c2], [b_out])

    def ln_stats(src_buf, src_tile):
        for c in range(DC):
            cx.op(qT, [src_buf, b_consts], [b_pb[4]],
                  lambda e, c=c: e.matmul(pb[4][0:4, :], ones4[:], src_tile(c),
                                          start=(c == 0), stop=(c == DC - 1)))
        for c in range(DC):
            sq = sqsc[:, (c % 2) * T:(c % 2) * T + T]
            cx.op(qV, [src_buf], [b_sq[c % 2]],
                  lambda e, c=c, sq=sq: e.tensor_tensor(
                      sq, src_tile(c), src_tile(c), Alu.mult))
            cx.op(qT, [b_sq[c % 2], b_consts], [b_pb[5]],
                  lambda e, c=c, sq=sq: e.matmul(
                      pb[5][0:4, :], ones4[:], sq,
                      start=(c == 0), stop=(c == DC - 1)))
        cx.op(qV, [b_pb[4]], [b_rm], lambda e: e.tensor_scalar(
            r_m[:], pb[4][0:1, :], 1.0 / D, None, Alu.mult))
        cx.op(qV, [b_pb[5]], [b_rt0], lambda e: e.tensor_scalar(
            r_t0[:], pb[5][0:1, :], 1.0 / D, None, Alu.mult))
        cx.op(qV, [b_rm], [b_rt1], lambda e: e.tensor_tensor(
            r_t1[:], r_m[:], r_m[:], Alu.mult))
        cx.op(qV, [b_rt0, b_rt1], [b_ram], lambda e: e.tensor_tensor(
            r_am[:], r_t0[:], r_t1[:], Alu.subtract))
        cx.op(qV, [b_ram], [b_rt1], lambda e: e.tensor_scalar(
            r_t1[:], r_am[:], EPS, None, Alu.add))
        cx.op(qS, [b_rt1], [b_rt0], lambda e: e.activation(
            r_t0[:], r_t1[:], AF.Sqrt))
        cx.op(qV, [b_rt0], [b_rs], lambda e: e.reciprocal(r_s[:], r_t0[:]))

    def bcast_row(row_h, row_buf, bank):
        def f(e):
            last = None
            for j in range(4):
                last = e.matmul(pb[bank][:, j * 128:(j + 1) * 128], ones1[:],
                                row_h[0:1, j * 128:(j + 1) * 128],
                                start=(j == 0), stop=(j == 3))
            return last
        cx.op(qT, [row_buf, b_consts], [b_pb[bank]], f)

    def amax_dance(mx_buf, mx_tile, width):
        nb = width // 128
        for j in range(nb):
            bk = 4 + (j % 2)
            cx.op(qT, [mx_buf, b_consts], [b_pb[bk]], lambda e, j=j, bk=bk:
                  e.transpose(pb[bk][:, 0:128],
                              mx_tile[:, j * 128:(j + 1) * 128], ident[:]))
            cx.op(qV, [b_pb[bk]], [b_atm], lambda e, j=j, bk=bk:
                  e.tensor_reduce(amax_tm[:, j:j + 1], pb[bk][:, 0:128],
                                  axis=AX, op=Alu.max,
                                  apply_absolute_value=True))
        cx.op(qT, [b_atm, b_consts], [b_pb[4]], lambda e: e.transpose(
            pb[4][0:4, 0:128], amax_tm[:], ident[:]))
        cx.op(qV, [b_pb[4]], [b_a4], lambda e: e.tensor_copy(
            a4[:], pb[4][0:4, 0:128]))
        dst = bass.AP(r_am, 0, [[T, 1], [1, width]])
        dma(qS, dqSC, dst, a4[0:nb, :], [b_a4], [b_ram])

    def wi_dma(g, q=None, dq=None):
        slot = g % 2
        for (dst_off, src_dram, bslab) in (
                (WIH[slot], win_hi, b_wih[slot]),
                (WIL[slot], win_lo, b_wil[slot])):
            for (blk, coloff) in ((0, g * 128), (1, INNER + g * 128)):
                src = bass.AP(src_dram, coloff,
                              [[F, 128], [F * 128, DC], [1, 128]])
                dst = bass.AP(arena, dst_off + blk * 128,
                              [[16384, 128], [256, DC], [1, 128]])
                dma(q or qSY, dq or dqSY, dst, src, [b_dwin], [bslab])

    def wo_dma(sl, q=None, dq=None):
        slot = sl % 2
        for (dst_off, src_dram, bslab) in (
                (WOH[slot], wout_hi, b_woh[slot]),
                (WOL[slot], wout_lo, b_wol[slot])):
            src = bass.AP(src_dram, sl * 2 * 128 * D,
                          [[D, 128], [D * 128, 2], [1, D]])
            dst = bass.AP(arena, dst_off, [[16384, 128], [1024, 2], [1, D]])
            dma(q or qSY, dq or dqSY, dst, src, [b_dwout], [bslab])

    def wi_split(g):
        slot = g % 2
        r = g % 4
        bwt = b_stg4[r]
        for (blk, coloff) in ((0, g * 128), (1, INNER + g * 128)):
            srcap = bass.AP(w_in, coloff, [[F, 128], [F * 128, DC], [1, 128]])
            dst = bass.AP(p_buf, r * 2048 + blk * 128,
                          [[8192, 128], [256, DC], [1, 128]])
            dma(qSY, dqSY, dst, srcap, [], [bwt])
        hi_ap = arena[:, WIH[slot]:WIH[slot] + 2048]
        lo_ap = arena[:, WIL[slot]:WIL[slot] + 2048]
        src_ap = p_buf[:, r * 2048:r * 2048 + 2048]
        cx.op(qS, [bwt], [b_wih[slot]], lambda e, hi_ap=hi_ap, src_ap=src_ap:
              e.copy(hi_ap, src_ap))
        cx.op(qV, [bwt, b_wih[slot]], [b_wil[slot]],
              lambda e, hi_ap=hi_ap, lo_ap=lo_ap, src_ap=src_ap:
              e.tensor_tensor(lo_ap, src_ap, hi_ap, Alu.subtract))
        for (blk, coloff) in ((0, g * 128), (1, INNER + g * 128)):
            dhi = bass.AP(win_hi, coloff, [[F, 128], [F * 128, DC], [1, 128]])
            dlo = bass.AP(win_lo, coloff, [[F, 128], [F * 128, DC], [1, 128]])
            shi = bass.AP(arena, WIH[slot] + blk * 128,
                          [[16384, 128], [256, DC], [1, 128]])
            slo = bass.AP(arena, WIL[slot] + blk * 128,
                          [[16384, 128], [256, DC], [1, 128]])
            dma(qS, dqSC, dhi, shi, [b_wih[slot]], [b_dwin])
            dma(qS, dqSC, dlo, slo, [b_wil[slot]], [b_dwin])

    def wo_split(sl):
        slot = sl % 2
        r = 2 + sl % 2
        bwt = b_stg4[r]
        srcap = bass.AP(w_out, sl * 2 * 128 * D, [[D, 128], [D * 128, 2], [1, D]])
        dst = bass.AP(p_buf, r * 2048, [[8192, 128], [1024, 2], [1, D]])
        dma(qSY, dqSY, dst, srcap, [], [bwt])
        hi_ap = arena[:, WOH[slot]:WOH[slot] + 2048]
        lo_ap = arena[:, WOL[slot]:WOL[slot] + 2048]
        src_ap = p_buf[:, r * 2048:r * 2048 + 2048]
        cx.op(qS, [bwt], [b_woh[slot]], lambda e, hi_ap=hi_ap, src_ap=src_ap:
              e.copy(hi_ap, src_ap))
        cx.op(qV, [bwt, b_woh[slot]], [b_wol[slot]],
              lambda e, hi_ap=hi_ap, lo_ap=lo_ap, src_ap=src_ap:
              e.tensor_tensor(lo_ap, src_ap, hi_ap, Alu.subtract))
        dhi = bass.AP(wout_hi, sl * 2 * 128 * D, [[D, 128], [D * 128, 2], [1, D]])
        dlo = bass.AP(wout_lo, sl * 2 * 128 * D, [[D, 128], [D * 128, 2], [1, D]])
        shi = bass.AP(arena, WOH[slot], [[16384, 128], [1024, 2], [1, D]])
        slo = bass.AP(arena, WOL[slot], [[16384, 128], [1024, 2], [1, D]])
        dma(qS, dqSC, dhi, shi, [b_woh[slot]], [b_dwout])
        dma(qS, dqSC, dlo, slo, [b_wol[slot]], [b_dwout])

    def wg_dma(m):
        slot = m % 2
        for (off, dram_t, bslab, bd) in (
                (WIH[slot], wgq, b_wih[slot], b_dram_q["g"]),
                (WIL[slot], wuq, b_wil[slot], b_dram_q["u"])):
            src = bass.AP(dram_t, 2 * m * 128,
                          [[FF, 128], [FF * 128, DC], [1, 256]])
            dst = bass.AP(arena, off, [[16384, 128], [256, DC], [1, 256]])
            dma(qG, dqG, dst, src, [bd], [bslab])

    def wg_quant_slab(m):
        for (key, src_dram, dst_dram, off, bslab, b_stg, stg_off) in (
                ("g", wgT, wgq, WIH[m % 2], b_wih[m % 2], b_stg_g, 0),
                ("u", wuT, wuq, WIL[m % 2], b_wil[m % 2], b_stg_u, 2048)):
            srcap = bass.AP(src_dram, 2 * m * 128,
                            [[FF, 128], [FF * 128, DC], [1, 256]])
            dst_in = bass.AP(c2buf, stg_off, [[4096, 128], [256, DC], [1, 256]])
            dma(qS, dqSC, dst_in, srcap, [], [b_stg])
            in_ap = c2buf[:, stg_off:stg_off + 2048]
            qi_ap = c2I[:, stg_off:stg_off + 2048]
            slot_ap = arena[:, off:off + 2048]
            cx.op(qS, [b_swp3[key]], [b_stg],
                  lambda e, in_ap=in_ap, qi_ap=qi_ap, key=key:
                  e.activation(qi_ap, in_ap, AF.Copy, scale=swp3[key][:]))
            cx.op(qV, [b_stg], [bslab],
                  lambda e, qi_ap=qi_ap, slot_ap=slot_ap:
                  e.tensor_scalar(slot_ap, qi_ap, -1.0, 1.0, Alu.max, Alu.min))
            dq_ap = bass.AP(dst_dram, 2 * m * 128,
                            [[FF, 128], [FF * 128, DC], [1, 256]])
            sl_ap = bass.AP(arena, off, [[16384, 128], [256, DC], [1, 256]])
            dma(qSY, dqSY, dq_ap, sl_ap, [bslab], [b_dram_q[key]])

    def wd_quant_slab(m):
        b_stg = b_stg_d[m % 2]
        stgF = xnlF if m % 2 == 0 else xnhF
        stgI = xnlI if m % 2 == 0 else xnhI
        slot = m % 2
        srcap = bass.AP(wdT, 2 * m * 128 * D, [[D, 128], [D * 128, 2], [1, D]])
        dst_in = bass.AP(stgF, 0, [[2048, 128], [1024, 2], [1, D]])
        dma(qS, dqSC, dst_in, srcap, [], [b_stg])
        slot_ap = arena[:, WOH[slot]:WOH[slot] + 2048]
        cx.op(qS, [b_swp3["d"]], [b_stg],
              lambda e, stgF=stgF, stgI=stgI:
              e.activation(stgI[:, 0:2048], stgF[:, 0:2048], AF.Copy,
                           scale=swp3["d"][:]))
        cx.op(qV, [b_stg], [b_woh[slot]],
              lambda e, stgI=stgI, slot_ap=slot_ap:
              e.tensor_scalar(slot_ap, stgI[:, 0:2048], -1.0, 1.0,
                              Alu.max, Alu.min))
        dq_ap = bass.AP(wdq, 2 * m * 128 * D, [[D, 128], [D * 128, 2], [1, D]])
        sl_ap = bass.AP(arena, WOH[slot], [[16384, 128], [1024, 2], [1, D]])
        dma(qSY, dqSY, dq_ap, sl_ap, [b_woh[slot]], [b_dram_q["d"], b_dwd[m]])

    def wd_dma(m):
        slot = m % 2
        src = bass.AP(wdq, 2 * m * 128 * D, [[D, 128], [D * 128, 2], [1, D]])
        dst = bass.AP(arena, WOH[slot], [[16384, 128], [1024, 2], [1, D]])
        dma(qG, dqG, dst, src, [b_dwd[m]], [b_woh[slot]])

    def p_ap(o):
        if o < 16:
            return p_buf[:, o * T:(o + 1) * T]
        return ybig32[:, (o - 16) * T:(o - 15) * T]

    def p_bufs(o):
        return b_p if o < 16 else b_ybig

    def ffn(t, pre_next=False):
        wg_fn = wg_quant_slab if t == 0 else wg_dma
        wd_fn = wd_quant_slab if t == 0 else wd_dma
        gu_units, dn_units = ([], [])
        if pre_next:
            gu_units, dn_units = pre_hook_units()
        if t == 0:
            for bb in (b_stg_g, b_stg_u):
                bb.r.append((qV, qV.count))
            b_stg_d[0].r.append((qT, qT.count))
        wg_fn(0)
        wg_fn(1)
        mxs = [sqsc[:, 0:T], sqsc[:, T:2 * T]]
        cur = 0
        for o in range(NO):
            slot = (o // 2) % 2
            gb, ub = 2 * (o % 2), 2 * (o % 2) + 1

            if o < 2:
                for c in range(DC):
                    def mm_gu1(e, o=o, slot=slot, gb=gb, ub=ub, c=c):
                        j = o % 2
                        lg = arena[:, WIH[slot] + c * 256 + j * 128:
                                   WIH[slot] + c * 256 + j * 128 + 128]
                        lu = arena[:, WIL[slot] + c * 256 + j * 128:
                                   WIL[slot] + c * 256 + j * 128 + 128]
                        rq = xnh[:, c * T:(c + 1) * T]
                        e.matmul(pb[gb][:], lg, rq,
                                 start=(c == 0), stop=(c == DC - 1))
                        return e.matmul(pb[ub][:], lu, rq,
                                        start=(c == 0), stop=(c == DC - 1))
                    cx.op(qT, [b_wih[slot], b_wil[slot], b_xq[c]],
                          [b_pb[gb], b_pb[ub]], mm_gu1)
            else:
                def mm_gu(e, o=o, slot=slot, gb=gb, ub=ub):
                    last = None
                    j = o % 2
                    for c in range(DC):
                        lg = arena[:, WIH[slot] + c * 256 + j * 128:
                                   WIH[slot] + c * 256 + j * 128 + 128]
                        lu = arena[:, WIL[slot] + c * 256 + j * 128:
                                   WIL[slot] + c * 256 + j * 128 + 128]
                        rq = xnh[:, c * T:(c + 1) * T]
                        e.matmul(pb[gb][:], lg, rq,
                                 start=(c == 0), stop=(c == DC - 1))
                        last = e.matmul(pb[ub][:], lu, rq,
                                        start=(c == 0), stop=(c == DC - 1))
                    return last
                cx.op(qT, [b_wih[slot], b_wil[slot]] + [b_xq[c] for c in range(DC)],
                      [b_pb[gb], b_pb[ub]], mm_gu)
            if o % 2 == 1 and o // 2 + 2 < NO // 2:
                wg_fn(o // 2 + 2)
            if gu_units and o % 2 == 0 and o >= 2:
                gu_units.pop(0)()
            cx.op(qV, [b_pb[gb], b_rg], [b_sg2],
                  lambda e, gb=gb: e.tensor_tensor(
                      sg2[:], pb[gb][:], rgsb[:], Alu.mult))
            cx.op(qS, [b_sg2], [b_sgs], lambda e:
                  e.activation(sgs[:], sg2[:], AF.Sigmoid))
            cx.op(qV, [b_sg2, b_sgs], [b_dsc], lambda e:
                  e.tensor_tensor(dsc[:], sg2[:], sgs[:], Alu.mult))
            cx.op(qV, [b_dsc, b_pb[ub]], [p_bufs(o)],
                  lambda e, o=o, ub=ub: e.tensor_tensor(
                      p_ap(o), dsc[:], pb[ub][:], Alu.mult))
            if o == 0:
                cx.op(qV, [p_bufs(o)], [b_sq[0]],
                      lambda e, o=o: e.scalar_tensor_tensor(
                          mxs[0], p_ap(o), -1.0, p_ap(o), Alu.mult, Alu.max))
                cur = 0
            else:
                cx.op(qV, [p_bufs(o), b_sq[cur]], [b_sq[1 - cur]],
                      lambda e, o=o, cur=cur: e.scalar_tensor_tensor(
                          mxs[1 - cur], p_ap(o), -1.0, mxs[cur],
                          Alu.mult, Alu.max))
                cur = 1 - cur
                cx.op(qV, [p_bufs(o), b_sq[cur]], [b_sq[1 - cur]],
                      lambda e, o=o, cur=cur: e.tensor_tensor(
                          mxs[1 - cur], p_ap(o), mxs[cur], Alu.max))
                cur = 1 - cur
        if t == 0:
            b_stg_d[1].r.append((qT, qT.count))
            wd_fn(0)
        amax_dance(b_sq[cur], mxs[cur], T)
        cx.op(qV, [b_ram, b_rru], [b_rt0], lambda e: e.tensor_tensor(
            r_t0[:], r_am[:], r_ru[:], Alu.mult))
        cx.op(qV, [b_rt0], [b_rt1], lambda e: e.tensor_scalar(
            r_t1[:], r_t0[:], 1e-5, None, Alu.max))
        cx.op(qV, [b_rt1], [b_rt0], lambda e: e.reciprocal(
            r_t0[:], r_t1[:]))
        cx.op(qV, [b_rt0], [b_ram], lambda e: e.tensor_scalar(
            r_am[:], r_t0[:], 127.0, None, Alu.mult))
        cx.op(qV, [b_ram, b_rru], [b_rt0], lambda e: e.tensor_tensor(
            r_t0[:], r_am[:], r_ru[:], Alu.mult))
        cx.op(qV, [b_rt1, b_wc["d"]], [b_ram], lambda e: e.tensor_scalar(
            r_am[:], r_t1[:], wc_d[:], None, Alu.mult))

        def f_bc(e):
            last = None
            for j in range(4):
                e.matmul(pb[6][:, j * 128:(j + 1) * 128], ones1[:],
                         r_t0[0:1, j * 128:(j + 1) * 128],
                         start=(j == 0), stop=(j == 3))
            for j in range(4):
                last = e.matmul(pb[7][:, j * 128:(j + 1) * 128],
                                ones1[:], r_am[0:1, j * 128:(j + 1) * 128],
                                start=(j == 0), stop=(j == 3))
            return last
        cx.op(qT, [b_rt0, b_ram, b_consts], [b_pb[6], b_pb[7]], f_bc)
        cx.op(qV, [b_pb[6]], [b_bsb], lambda e: e.tensor_copy(
            bsb[:], pb[6][:]))
        cx.op(qV, [b_pb[7]], [b_rd], lambda e: e.tensor_copy(
            rdsb[:], pb[7][:]))
        if pre_next:
            wi_dma(0)
            wi_dma(1)
        if t != 0:
            wd_fn(0)
        wd_fn(1)
        for o in range(NO):
            h = o % 2
            slot = (o // 2) % 2
            cx.op(qV, [p_bufs(o), b_bsb], [b_qi[0]], lambda e, o=o:
                  e.tensor_tensor(qisc[:], p_ap(o), bsb[:], Alu.mult))
            cx.op(qV, [b_qi[0]], [b_qr[h]], lambda e, h=h: e.tensor_scalar(
                qring[:, h * T:(h + 1) * T], qisc[:],
                -128.0, 127.0, Alu.max, Alu.min))

            def mm_d(e, o=o, h=h, slot=slot):
                last = None
                j = o % 2
                for d in range(8):
                    lw = arena[:, WOH[slot] + j * 1024 + d * 128:
                               WOH[slot] + j * 1024 + d * 128 + 128]
                    last = e.matmul(pb[d][:], lw,
                                    qring[:, h * T:(h + 1) * T],
                                    start=(o == 0), stop=(o == NO - 1))
                return last
            cx.op(qT, [b_woh[slot], b_qr[h]],
                  [b_pb[d] for d in range(8)], mm_d)
            if o % 2 == 1 and o // 2 + 2 < NO // 2:
                wd_fn(o // 2 + 2)
            if dn_units and o % 4 == 0:
                dn_units.pop(0)()
        if t == 0:
            b_c2.r.append((qS, qS.count))
            for bb in (b_xnh, b_xnl):
                bb.r.append((qS, qS.count))
                bb.r.append((qV, qV.count))
        for d in range(DC):
            cx.op(qV, [b_pb[d], b_rd], [b_dsc], lambda e, d=d:
                  e.tensor_tensor(dsc[:], pb[d][:], rdsb[:], Alu.mult))
            cx.op(qV, [b_dsc, b_xr], [b_c2],
                  lambda e, d=d: e.tensor_tensor(
                      c2buf[:, d * T:(d + 1) * T], dsc[:],
                      xrs(d), Alu.add))

    def tile_mixer_pre(t):
        ln_stats(b_x, xs)
        bcast_row(r_m, b_rm, 6)
        cx.op(qV, [b_pb[6]], [b_m1], lambda e: e.tensor_copy(m1sb[:], pb[6][:]))
        for c in range(DC):
            cx.op(qV, [b_x, b_m1], [b_c2], lambda e, c=c: e.tensor_tensor(
                c2buf[:, c * T:(c + 1) * T], xs(c), m1sb[:], Alu.subtract))
        bcast_row(r_s, b_rs, 7)
        cx.op(qV, [b_pb[7]], [b_s1], lambda e: e.tensor_copy(s1sb[:], pb[7][:]))
        xn2v = sqsc[:, 0:T]
        for c in range(DC):
            cx.op(qV, [b_c2, b_s1], [b_sq[0]], lambda e, c=c, xn2v=xn2v:
                  e.tensor_tensor(xn2v, c2buf[:, c * T:(c + 1) * T],
                                  s1sb[:], Alu.mult))
            cx.op(qV, [b_sq[0]], [b_xnh, b_xq[c]], lambda e, c=c, xn2v=xn2v:
                  e.tensor_copy(xnh[:, c * T:(c + 1) * T], xn2v))
            cx.op(qV, [b_sq[0], b_xnh], [b_xnl], lambda e, c=c, xn2v=xn2v:
                  e.tensor_tensor(xnl[:, c * T:(c + 1) * T], xn2v,
                                  xnh[:, c * T:(c + 1) * T], Alu.subtract))


    def pre_hook_units():
        gu = []

        def u_sums():
            def f(e):
                last = None
                for c in range(DC):
                    last = e.matmul(pb[4][0:4, :], ones4[:], xs(c),
                                    start=(c == 0), stop=(c == DC - 1))
                return last
            cx.op(qT, [b_x, b_consts], [b_pb[4]], f)
        gu.append(u_sums)

        def u_sumsq(c):
            def emit():
                cx.op(qV, [b_x], [b_xnsc], lambda e, c=c: e.tensor_tensor(
                    xnsc[:], xs(c), xs(c), Alu.mult))
                cx.op(qT, [b_xnsc, b_consts], [b_pb[5]],
                      lambda e, c=c: e.matmul(pb[5][0:4, :], ones4[:],
                                              xnsc[:], start=(c == 0),
                                              stop=(c == DC - 1)))
            return emit
        for c in range(DC):
            gu.append(u_sumsq(c))

        def u_rows():
            cx.op(qV, [b_pb[4]], [b_rm], lambda e: e.tensor_scalar(
                r_m[:], pb[4][0:1, :], 1.0 / D, None, Alu.mult))
            cx.op(qV, [b_pb[5]], [b_rt0], lambda e: e.tensor_scalar(
                r_t0[:], pb[5][0:1, :], 1.0 / D, None, Alu.mult))
            cx.op(qV, [b_rm], [b_rt1], lambda e: e.tensor_tensor(
                r_t1[:], r_m[:], r_m[:], Alu.mult))
            cx.op(qV, [b_rt0, b_rt1], [b_ram], lambda e: e.tensor_tensor(
                r_am[:], r_t0[:], r_t1[:], Alu.subtract))
            cx.op(qV, [b_ram], [b_rt1], lambda e: e.tensor_scalar(
                r_t1[:], r_am[:], EPS, None, Alu.add))
            cx.op(qS, [b_rt1], [b_rt0], lambda e: e.activation(
                r_t0[:], r_t1[:], AF.Sqrt))
            cx.op(qV, [b_rt0], [b_rs], lambda e: e.reciprocal(r_s[:], r_t0[:]))
        gu.append(u_rows)

        def u_bc_m():
            bcast_row(r_m, b_rm, 6)
            cx.op(qV, [b_pb[6]], [b_m1],
                  lambda e: e.tensor_copy(m1sb[:], pb[6][:]))
        gu.append(u_bc_m)

        def u_bc_s():
            bcast_row(r_s, b_rs, 7)
            cx.op(qV, [b_pb[7]], [b_s1],
                  lambda e: e.tensor_copy(s1sb[:], pb[7][:]))
        gu.append(u_bc_s)

        dn = []

        def u_xn(c):
            def emit():
                xn2v = sqsc[:, 0:T]
                cx.op(qV, [b_x, b_m1], [b_c2], lambda e, c=c: e.tensor_tensor(
                    c2buf[:, c * T:(c + 1) * T], xs(c), m1sb[:],
                    Alu.subtract))
                cx.op(qV, [b_c2, b_s1], [b_sq[0]], lambda e, c=c, xn2v=xn2v:
                      e.tensor_tensor(xn2v, c2buf[:, c * T:(c + 1) * T],
                                      s1sb[:], Alu.mult))
                cx.op(qV, [b_sq[0]], [b_xnh, b_xq[c]],
                      lambda e, c=c, xn2v=xn2v: e.tensor_copy(
                          xnh[:, c * T:(c + 1) * T], xn2v))
                cx.op(qV, [b_sq[0], b_xnh], [b_xnl],
                      lambda e, c=c, xn2v=xn2v: e.tensor_tensor(
                          xnl[:, c * T:(c + 1) * T], xn2v,
                          xnh[:, c * T:(c + 1) * T], Alu.subtract))
            return emit
        for c in range(DC):
            dn.append(u_xn(c))
        return gu, dn

    def tile_mixer_main(t, skip_slabs=False):
        wi_fn = wi_split if t == 0 else wi_dma
        wo_fn = wo_split if t == 0 else wo_dma
        if not skip_slabs:
            wi_fn(0)
            wi_fn(1)
        for g in range(NA):
            slot = g % 2
            ba, bg = 2 * (g % 4), 2 * (g % 4) + 1

            def mm_u(e, slot=slot, ba=ba, bg=bg):
                last = None
                for c in range(DC):
                    for (bank, blk) in ((ba, 0), (bg, 1)):
                        lh = arena[:, WIH[slot] + c * 256 + blk * 128:
                                   WIH[slot] + c * 256 + blk * 128 + 128]
                        ll = arena[:, WIL[slot] + c * 256 + blk * 128:
                                   WIL[slot] + c * 256 + blk * 128 + 128]
                        rh = xnh[:, c * T:(c + 1) * T]
                        rl = xnl[:, c * T:(c + 1) * T]
                        e.matmul(pb[bank][:], lh, rh, start=(c == 0), stop=False)
                        e.matmul(pb[bank][:], lh, rl, start=False, stop=False)
                        last = e.matmul(pb[bank][:], ll, rh, start=False,
                                        stop=(c == DC - 1))
                return last
            cx.op(qT, [b_wih[slot], b_wil[slot], b_xnh, b_xnl],
                  [b_pb[ba], b_pb[bg]], mm_u)
            sg_half = sigb[:, 0:T]
            cx.op(qS, [b_pb[bg]], [b_sig],
                  lambda e, bg=bg, sg_half=sg_half:
                  e.activation(sg_half, pb[bg][:], AF.Sigmoid))
            cx.op(qV, [b_carry[g]], [b_glu], lambda e, g=g: e.tensor_copy(
                glu[:, 0:3], carry[:, 3 * g:3 * g + 3]))
            cx.op(qV, [b_pb[ba], b_sig, b_glu], [b_glu],
                  lambda e, ba=ba, sg_half=sg_half: e.tensor_tensor(
                      glu[:, 3:3 + T], pb[ba][:], sg_half, Alu.mult))
            cx.op(qV, [b_glu], [b_carry[g]], lambda e, g=g: e.tensor_copy(
                carry[:, 3 * g:3 * g + 3], glu[:, T:T + 3]))
            ya, yb = yact[:, 0:T], yact[:, T:2 * T]
            cx.op(qV, [b_glu, b_consts], [b_yact[0]], lambda e, g=g, ya=ya:
                  e.tensor_scalar(ya, glu[:, 0:T],
                                  convw[:, 4 * g:4 * g + 1],
                                  convb[:, g:g + 1], Alu.mult, Alu.add))
            cur_t, oth_t = yb, ya
            buf_c, buf_o = b_yact[1], b_yact[0]
            for k in range(1, 4):
                cx.op(qV, [b_glu, b_consts, buf_o], [buf_c],
                      lambda e, g=g, k=k, ct=cur_t, ot=oth_t:
                      e.scalar_tensor_tensor(
                          ct, glu[:, k:k + T],
                          convw[:, 4 * g + k:4 * g + k + 1],
                          ot, Alu.mult, Alu.add))
                cur_t, oth_t = oth_t, cur_t
                buf_c, buf_o = buf_o, buf_c
            cx.op(qS, [buf_o], [b_sig], lambda e, rt=oth_t:
                  e.activation(sigb[:, 0:T], rt, AF.Sigmoid))
            cx.op(qV, [buf_o, b_sig], [buf_c],
                  lambda e, rt=oth_t, ot2=cur_t:
                  e.tensor_tensor(ot2, rt, sigb[:, 0:T], Alu.mult))
            cx.op(qV, [buf_c], [b_ybig], lambda e, g=g, ot2=cur_t: e.tensor_copy(
                yh[:, g * T:(g + 1) * T], ot2))
            cx.op(qV, [buf_c, b_ybig], [b_ybig], lambda e, g=g, ot2=cur_t:
                  e.tensor_tensor(yl[:, g * T:(g + 1) * T], ot2,
                                  yh[:, g * T:(g + 1) * T], Alu.subtract))
            if g + 2 < NA:
                wi_fn(g + 2)
            elif g == NA - 2:
                wo_fn(0)
                wo_fn(1)
        if t == 0:
            for bb in (b_wtmpA, b_wtmpB):
                bb.r.append((qS, qS.count))
                bb.r.append((qV, qV.count))

        for f_ in range(NA):
            slot = (f_ // 2) % 2

            def mm_o(e, f_=f_, slot=slot):
                last = None
                j = f_ % 2
                for d in range(DC):
                    lh = arena[:, WOH[slot] + j * 1024 + d * 128:
                               WOH[slot] + j * 1024 + d * 128 + 128]
                    ll = arena[:, WOL[slot] + j * 1024 + d * 128:
                               WOL[slot] + j * 1024 + d * 128 + 128]
                    rh = yh[:, f_ * T:(f_ + 1) * T]
                    rl = yl[:, f_ * T:(f_ + 1) * T]
                    e.matmul(pb[d][:], lh, rh, start=(f_ == 0), stop=False)
                    e.matmul(pb[d][:], lh, rl, start=False, stop=False)
                    last = e.matmul(pb[d][:], ll, rh, start=False,
                                    stop=(f_ == NA - 1))
                return last
            cx.op(qT, [b_woh[slot], b_wol[slot], b_ybig],
                  [b_pb[d] for d in range(8)], mm_o)
            if f_ % 2 == 1 and f_ // 2 + 2 < 8:
                wo_fn(f_ // 2 + 2)
        for d in range(DC):
            cx.op(qV, [b_pb[d], b_x], [b_xr], lambda e, d=d: e.tensor_tensor(
                xrs(d), pb[d][:], xs(d), Alu.add))
        if t + 1 < n_tiles:
            x_load(t + 1)


    def tile_ffn(t, pre_next=False):
        ln_stats(b_xr, xrs)
        bcast_row(r_m, b_rm, 6)
        cx.op(qV, [b_pb[6]], [b_m1], lambda e: e.tensor_copy(m1sb[:], pb[6][:]))
        mxs = [sqsc[:, 0:T], sqsc[:, T:2 * T]]
        cur = 0
        for c in range(DC):
            cx.op(qV, [b_xr, b_m1], [b_c2], lambda e, c=c: e.tensor_tensor(
                c2buf[:, c * T:(c + 1) * T], xrs(c), m1sb[:], Alu.subtract))
            if c == 0:
                cx.op(qV, [b_c2], [b_sq[0]], lambda e: e.scalar_tensor_tensor(
                    mxs[0], c2buf[:, 0:T], -1.0, c2buf[:, 0:T],
                    Alu.mult, Alu.max))
            else:
                cx.op(qV, [b_c2, b_sq[cur]], [b_sq[1 - cur]],
                      lambda e, c=c, cur=cur: e.scalar_tensor_tensor(
                          mxs[1 - cur], c2buf[:, c * T:(c + 1) * T], -1.0,
                          mxs[cur], Alu.mult, Alu.max))
                cur = 1 - cur
                cx.op(qV, [b_c2, b_sq[cur]], [b_sq[1 - cur]],
                      lambda e, c=c, cur=cur: e.tensor_tensor(
                          mxs[1 - cur], c2buf[:, c * T:(c + 1) * T],
                          mxs[cur], Alu.max))
                cur = 1 - cur
        amax_dance(b_sq[cur], mxs[cur], T)
        cx.op(qV, [b_ram, b_rs], [b_rt0], lambda e: e.tensor_tensor(
            r_t0[:], r_am[:], r_s[:], Alu.mult))
        cx.op(qV, [b_rt0], [b_rt1], lambda e: e.tensor_scalar(
            r_t1[:], r_t0[:], 1e-5, None, Alu.max))
        cx.op(qV, [b_rt1], [b_rt0], lambda e: e.reciprocal(r_t0[:], r_t1[:]))
        cx.op(qV, [b_rt0], [b_ram], lambda e: e.tensor_scalar(
            r_am[:], r_t0[:], 127.0, None, Alu.mult))
        cx.op(qV, [b_ram, b_rs], [b_rt0], lambda e: e.tensor_tensor(
            r_t0[:], r_am[:], r_s[:], Alu.mult))
        bcast_row(r_t0, b_rt0, 7)
        cx.op(qV, [b_pb[7]], [b_a2], lambda e: e.tensor_copy(a2sb[:], pb[7][:]))
        cx.op(qV, [b_rt1, b_wc["g"]], [b_ram], lambda e: e.tensor_scalar(
            r_am[:], r_t1[:], wc_g[:], None, Alu.mult))
        cx.op(qV, [b_rt1, b_wc["u"]], [b_rru], lambda e: e.tensor_scalar(
            r_ru[:], r_t1[:], wc_u[:], None, Alu.mult))
        bcast_row(r_am, b_ram, 6)
        cx.op(qV, [b_pb[6]], [b_rg], lambda e: e.tensor_copy(rgsb[:], pb[6][:]))
        for c in range(DC):
            cx.op(qV, [b_c2, b_a2], [b_qi[0]], lambda e, c=c:
                  e.tensor_tensor(qisc[:], c2buf[:, c * T:(c + 1) * T],
                                  a2sb[:], Alu.mult))
            cx.op(qV, [b_qi[0], b_xnh], [b_xq[c]], lambda e, c=c:
                  e.tensor_scalar(xnh[:, c * T:(c + 1) * T], qisc[:],
                                  -128.0, 127.0, Alu.max, Alu.min))

        ffn(t, pre_next)

        out_store(t)

    x_load(0)
    tile_mixer_pre(0)

    for t in range(n_tiles):
        if t == 0:
            tile_mixer_main(0)
            quant_pass1(("g",))
            quant_rows("g")
            quant_pass1(("u",))
            quant_rows("u")
            quant_pass1(("d",))
            quant_rows("d")
            f2 = [(qV, qV.count), (qT, qT.count), (qS, qS.count)]
            for sq_ in list(dqG) + list(dqSC):
                if sq_.count:
                    f2.append((sq_, sq_.count))
            b_p.r.extend(f2)
            tile_ffn(0)
        else:
            if t == 1:
                tile_mixer_pre(t)
            tile_mixer_main(t, skip_slabs=(t >= 2))
            tile_ffn(t, pre_next=(t < n_tiles - 1))

    for i, sq_ in enumerate(dqSY):
        if sq_.count:
            thSY.append(lambda e, s=sq_.sem, c=sq_.count: e.wait_ge(s, c))
    for i, sq_ in enumerate(dqG):
        if sq_.count:
            thG.append(lambda e, s=sq_.sem, c=sq_.count: e.wait_ge(s, c))
    for i, sq_ in enumerate(dqSC):
        if sq_.count:
            thS.append(lambda e, s=sq_.sem, c=sq_.count: e.wait_ge(s, c))

    with nc.Block() as block:
        @block.tensor
        def _(eng):
            for th in thT:
                th(eng)

        @block.vector
        def _(eng):
            for th in thV:
                th(eng)

        @block.scalar
        def _(eng):
            for th in thS:
                th(eng)

        @block.gpsimd
        def _(eng):
            for th in thG:
                th(eng)

        @block.sync
        def _(eng):
            for th in thSY:
                th(eng)

    return nc


def kernel(L0_attn_x, L0_attn_W_in, L0_attn_conv_w, L0_attn_conv_b,
           L0_attn_W_out, L0_ffn_w_gate, L0_ffn_w_up, L0_ffn_w_down):
    x = np.asarray(L0_attn_x)
    W_in = np.ascontiguousarray(np.asarray(L0_attn_W_in))
    W_out = np.ascontiguousarray(np.asarray(L0_attn_W_out))
    conv_w = np.asarray(L0_attn_conv_w)
    conv_b = np.asarray(L0_attn_conv_b)
    wgT_np = np.ascontiguousarray(np.asarray(L0_ffn_w_gate).T)
    wuT_np = np.ascontiguousarray(np.asarray(L0_ffn_w_up).T)
    wdT_np = np.ascontiguousarray(np.asarray(L0_ffn_w_down).T)
    convw_np = np.ascontiguousarray(
        conv_w.reshape(16, 128, 4).transpose(1, 0, 2).reshape(128, 64))
    convb_np = np.ascontiguousarray(conv_b.reshape(16, 128).T)

    nc = build_program(n_tiles=8)
    in_maps = []
    for b in range(8):
        xT_b = np.ascontiguousarray(x[b].T)
        in_maps.append({
            "xT": xT_b, "w_in": W_in, "w_out": W_out,
            "convw": convw_np, "convb": convb_np,
            "wgT": wgT_np, "wuT": wuT_np, "wdT": wdT_np,
        })
    res = run_bass_kernel_spmd(nc, in_maps, core_ids=list(range(8)))
    out = np.stack([np.asarray(res.results[b]["outT"]).T for b in range(8)],
                   axis=0)
    return np.ascontiguousarray(out, dtype=np.float32)


# version 21
# speedup vs baseline: 1.0100x; 1.0099x over previous
import numpy as np
import concourse.bass as bass
import concourse.mybir as mybir
from concourse.bass_utils import run_bass_kernel_spmd
from concourse.alu_op_type import AluOpType as Alu

F32 = mybir.dt.float32
BF16 = mybir.dt.bfloat16
I32 = mybir.dt.int32
AF = mybir.ActivationFunctionType
AX = mybir.AxisListType.X

D = 1024
DC = 8
F = 4096
INNER = 2048
NA = 16
FF = 4096
NO = 32
T = 512
SUB = 256
L = 4096
EPS = 1e-5


class _Q:
    __slots__ = ("name", "sem", "step", "count", "thunks")

    def __init__(self, name, sem, step, thunks):
        self.name, self.sem, self.step = name, sem, step
        self.count = 0
        self.thunks = thunks


class _Buf:
    __slots__ = ("w", "r")

    def __init__(self):
        self.w = None
        self.r = []


class _Ctx:
    def __init__(self):
        self.known = {}

    def op(self, q, ins, outs, fn, dq=None):
        deps = []
        if dq is not None and dq.count > 0:
            deps.append((dq, dq.count))
        for b in ins:
            if b.w is not None:
                deps.append(b.w)
        for b in outs:
            if b.w is not None:
                deps.append(b.w)
            deps.extend(b.r)
        need = {}
        for (tq, cnt) in deps:
            if need.get(tq.name, (None, 0))[1] < cnt:
                need[tq.name] = (tq, cnt)
        for (tq, cnt) in need.values():
            key = (q.name, tq.name)
            if self.known.get(key, 0) < cnt:
                sem = tq.sem
                q.thunks.append(lambda e, sem=sem, cnt=cnt: e.wait_ge(sem, cnt))
                self.known[key] = cnt
        tq = dq if dq is not None else q
        sem, step = tq.sem, tq.step
        q.thunks.append(lambda e, fn=fn, sem=sem, step=step:
                        fn(e).then_inc(sem, step))
        tq.count += tq.step
        c = (tq, tq.count)
        for b in ins:
            b.r.append(c)
        for b in outs:
            b.w = c
            b.r = []
        return c


def build_program(n_tiles=8):
    nc = bass.Bass("TRN2", target_bir_lowering=False)

    xT = nc.dram_tensor("xT", [D, L], F32, kind="ExternalInput")
    w_in = nc.dram_tensor("w_in", [D, F], F32, kind="ExternalInput")
    w_out = nc.dram_tensor("w_out", [INNER, D], F32, kind="ExternalInput")
    convw_d = nc.dram_tensor("convw", [128, 64], F32, kind="ExternalInput")
    convb_d = nc.dram_tensor("convb", [128, 16], F32, kind="ExternalInput")
    wgT = nc.dram_tensor("wgT", [D, FF], F32, kind="ExternalInput")
    wuT = nc.dram_tensor("wuT", [D, FF], F32, kind="ExternalInput")
    wdT = nc.dram_tensor("wdT", [FF, D], F32, kind="ExternalInput")
    outT = nc.dram_tensor("outT", [D, L], F32, kind="ExternalOutput")
    win_hi = nc.dram_tensor("win_hi", [D, F], BF16, kind="Internal")
    win_lo = nc.dram_tensor("win_lo", [D, F], BF16, kind="Internal")
    wout_hi = nc.dram_tensor("wout_hi", [INNER, D], BF16, kind="Internal")
    wout_lo = nc.dram_tensor("wout_lo", [INNER, D], BF16, kind="Internal")
    wgq = nc.dram_tensor("wgq", [D, FF], BF16, kind="Internal")
    wuq = nc.dram_tensor("wuq", [D, FF], BF16, kind="Internal")
    wdq = nc.dram_tensor("wdq", [FF, D], BF16, kind="Internal")

    A = nc.alloc_sbuf_tensor
    ones1 = A("ones1", [1, 128], F32)
    ones4 = A("ones4", [128, 4], F32)
    ident = A("ident", [128, 128], F32)
    iota_p = A("iota_p", [128, 128], F32)
    iota_f = A("iota_f", [128, 128], F32)
    convw = A("convw_s", [128, 64], F32)
    convb = A("convb_s", [128, 16], F32)
    carry = A("carry", [128, 48], F32)
    xbuf = A("xbuf", [128, DC * T], F32)
    xrbuf = A("xrbuf", [128, DC * T], F32)
    c2buf = A("c2buf", [128, DC * T], F32)
    sqsc = A("sqsc", [128, 2 * T], F32)
    xnsc = A("xnsc", [128, T], F32)
    xnh = A("xnh", [128, DC * T], BF16)
    xnl = A("xnl", [128, DC * T], BF16)
    glu = A("glu", [128, T + 3], F32)
    sigb = A("sigb", [128, T], F32)
    yact = A("yact", [128, 2 * T], F32)
    ybig = A("ybig", [128, 2 * NA * T], BF16)
    yh = ybig[:, 0:NA * T]
    yl = ybig[:, NA * T:2 * NA * T]
    ybig32 = ybig.bitcast(F32)
    p_buf = A("p_buf", [128, 16 * T], F32)
    qisc = A("qisc", [128, T], I32)
    qring = A("qring", [128, 2 * T], BF16)
    sg2 = A("sg2", [128, T], F32)
    sgs = A("sgs", [128, T], F32)
    dsc = A("dsc", [128, T], F32)
    m1sb = A("m1sb", [128, T], F32)
    s1sb = A("s1sb", [128, T], F32)
    a2sb = A("a2sb", [128, T], F32)
    rgsb = A("rgsb", [128, T], F32)
    bsb = a2sb
    rdsb = rgsb
    r_m = A("r_m", [1, T], F32)
    r_s = A("r_s", [1, T], F32)
    r_t0 = A("r_t0", [1, T], F32)
    r_t1 = A("r_t1", [1, T], F32)
    r_am = A("r_am", [1, T], F32)
    r_ru = A("r_ru", [1, T], F32)
    amax_tm = A("amax_tm", [128, 4], F32)
    a4 = A("a4", [4, 128], F32)
    r11a = A("r11a", [1, 1], F32)
    r11b = A("r11b", [1, 1], F32)
    r11c = A("r11c", [1, 1], F32)
    swp = A("swp", [128, 1], F32)
    wc_g = A("wc_g", [1, 1], F32)
    wc_u = A("wc_u", [1, 1], F32)
    wc_d = A("wc_d", [1, 1], F32)
    arena = A("arena", [128, 16384], BF16)
    arena32 = arena.bitcast(F32)
    arenaI = arena.bitcast(I32)

    WIH = [0, 2048]
    WIL = [4096, 6144]
    WOH = [8192, 10240]
    WOL = [12288, 14336]
    wtmpA = arena32[:, 0:2048]
    wtmpB = arena32[:, 2048:4096]
    hi_t = arena[:, 8192:10240]
    lo_t = arena[:, 10240:12288]
    p_bufI = p_buf.bitcast(I32)
    p_bufB = p_buf.bitcast(BF16)
    qtmpA = p_buf[:, 0:2048]
    qtmpB = p_buf[:, 2048:4096]
    abs_t = p_buf[:, 4096:6144]
    qi_t = p_bufI[:, 4096:6144]
    qb_t = p_bufB[:, 12288:14336]
    c2I = c2buf.bitcast(I32)
    xnhF = xnh.bitcast(F32)
    xnhI = xnh.bitcast(I32)
    xnlF = xnl.bitcast(F32)
    xnlI = xnl.bitcast(I32)

    pb = [nc.alloc_psum_tensor("pb%d" % i, [128, 512], F32) for i in range(8)]

    sT = nc.alloc_semaphore("sT")
    sV = nc.alloc_semaphore("sV")
    sS = nc.alloc_semaphore("sS")
    sG = nc.alloc_semaphore("sG")
    sdSY = [nc.alloc_semaphore("sdSY%d" % i) for i in range(6)]
    sdSC = [nc.alloc_semaphore("sdSC%d" % i) for i in range(2)]
    sdG = [nc.alloc_semaphore("sdG%d" % i) for i in range(4)]

    thT, thV, thS, thG, thSY = [], [], [], [], []
    qT = _Q("T", sT, 1, thT)
    qV = _Q("V", sV, 1, thV)
    qS = _Q("S", sS, 1, thS)
    qG = _Q("G", sG, 1, thG)
    qSY = _Q("SY", None, 1, thSY)
    dqSY = [_Q("dSY%d" % i, sdSY[i], 16, None) for i in range(6)]
    dqSC = [_Q("dSC%d" % i, sdSC[i], 16, None) for i in range(2)]
    dqG = [_Q("dG%d" % i, sdG[i], 16, None) for i in range(4)]
    rot = {"SY": [0], "SC": [0], "G": [0]}

    cx = _Ctx()
    B = _Buf
    b_x = B(); b_xr = B(); b_c2 = B()
    b_sq = [B(), B()]
    b_xnsc = B()
    b_xnh = B(); b_xnl = B()
    b_xq = [B() for _ in range(DC)]
    b_glu = B(); b_sig = B(); b_yact = [B(), B()]
    b_ybig = B()
    b_p = B(); b_qi = [B(), B()]; b_qr = [B(), B()]
    b_sg2 = B(); b_sgs = B(); b_dsc = B()
    b_m1 = B(); b_s1 = B(); b_a2 = B(); b_rg = B()
    b_bsb = b_a2; b_rd = b_rg
    b_rm = B(); b_rs = B(); b_rt0 = B(); b_rt1 = B()
    b_ram = B(); b_rru = B()
    b_atm = B(); b_a4 = B()
    b_r11a = B(); b_r11b = B(); b_r11c = B(); b_swp = B()
    b_wc = {"g": B(), "u": B(), "d": B()}
    wc_t = {"g": wc_g, "u": wc_u, "d": wc_d}
    b_carry = [B() for _ in range(NA)]
    b_consts = B()
    b_pb = [B() for _ in range(8)]
    b_wih = [B(), B()]; b_wil = [B(), B()]
    b_woh = [B(), B()]; b_wol = [B(), B()]
    b_wtmpA = B(); b_wtmpB = B(); b_hi = B(); b_lo = B()
    b_qit = B(); b_qbt = B()
    b_dwin = B(); b_dwout = B()
    b_dram_q = {"g": B(), "u": B(), "d": B()}
    b_dwd = [B() for _ in range(16)]
    b_stg_g = B(); b_stg_u = B(); b_stg_d = [B(), B()]
    b_stg4 = [B() for _ in range(4)]
    b_out = B()

    def dma(q, dqp, dst_ap, src_ap, ins, outs):
        if dqp is dqSY:
            key = "SY"
        elif dqp is dqSC:
            key = "SC"
        else:
            key = "G"
        i = rot[key][0]
        rot[key][0] = (i + 1) % len(dqp)
        return cx.op(q, ins, outs,
                     lambda e, d=dst_ap, s=src_ap: e.dma_start(out=d, in_=s),
                     dq=dqp[i])

    cx.op(qG, [], [b_consts], lambda e: e.memset(ones1[:], 1.0))
    cx.op(qG, [b_consts], [b_consts], lambda e: e.memset(ones4[:], 1.0))
    cx.op(qG, [b_consts], [b_consts], lambda e: e.memset(carry[:], 0.0))
    cx.op(qG, [b_consts], [b_consts], lambda e: e.memset(amax_tm[:], 0.0))
    cx.op(qG, [b_consts], [b_consts], lambda e: e.iota(
        iota_p[:], [[0, 128]], channel_multiplier=1,
        allow_small_or_imprecise_dtypes=True))
    cx.op(qG, [b_consts], [b_consts], lambda e: e.iota(
        iota_f[:], [[1, 128]], channel_multiplier=0,
        allow_small_or_imprecise_dtypes=True))
    cx.op(qV, [b_consts], [b_consts], lambda e: e.tensor_tensor(
        ident[:], iota_p[:], iota_f[:], Alu.is_equal))
    dma(qSY, dqSY, convw[:], convw_d.ap(), [b_consts], [b_consts])
    dma(qSY, dqSY, convb[:], convb_d.ap(), [b_consts], [b_consts])
    for i in range(NA):
        b_carry[i].w = b_consts.w

    def split_hilo(src_dram, dst_hi, dst_lo, rows, cols, b_dst):
        nch = rows // 128
        nh = cols // 2048 if cols >= 2048 else 1
        w = cols // nh
        n_it = nch * nh
        slots = [(wtmpA, b_wtmpA), (wtmpB, b_wtmpB)]

        def src_ap(i):
            c, h = divmod(i, nh)
            return bass.AP(src_dram, c * 128 * cols + h * w,
                           [[cols, 128], [1, w]])

        def dst_ap(dst, i):
            c, h = divmod(i, nh)
            return bass.AP(dst, c * 128 * cols + h * w, [[cols, 128], [1, w]])

        dma(qSY, dqSY, slots[0][0][:, 0:w], src_ap(0), [], [slots[0][1]])
        for i in range(n_it):
            wt, bwt = slots[i % 2]
            if i + 1 < n_it:
                wt2, bwt2 = slots[(i + 1) % 2]
                dma(qSY, dqSY, wt2[:, 0:w], src_ap(i + 1), [], [bwt2])
            cx.op(qS, [bwt], [b_hi], lambda e, wt=wt, w=w:
                  e.copy(hi_t[:, 0:w], wt[:, 0:w]))
            cx.op(qV, [bwt, b_hi], [b_lo], lambda e, wt=wt, w=w:
                  e.tensor_tensor(lo_t[:, 0:w], wt[:, 0:w],
                                  hi_t[:, 0:w], Alu.subtract))
            dma(qSY, dqSY, dst_ap(dst_hi, i), hi_t[:, 0:w], [b_hi], [b_dst])
            dma(qSY, dqSY, dst_ap(dst_lo, i), lo_t[:, 0:w], [b_lo], [b_dst])

    QW = {
        "g": (wgT, wgq, D, FF, 4, dqG),
        "u": (wuT, wuq, D, FF, 5, dqG),
        "d": (wdT, wdq, FF, D, 6, dqSC),
    }
    QQ = {"g": qG, "u": qG, "d": qS}
    swp3 = {k: A("swp_" + k, [128, 1], F32) for k in ("g", "u", "d")}
    b_swp3 = {k: B() for k in ("g", "u", "d")}

    def q_iters(key):
        rows_, cols_ = QW[key][2], QW[key][3]
        nch = rows_ // 128
        nh = max(1, cols_ // 2048)
        return nch * nh, cols_ // nh, nch, nh

    def q_src(key, i, dram_idx):
        dram_t = QW[key][dram_idx]
        n_it, w, nch, nh = q_iters(key)
        c, h = divmod(i, nh)
        return bass.AP(dram_t, c * 128 * QW[key][3] + h * w,
                       [[QW[key][3], 128], [1, w]])

    def quant_pass1(keys=("g", "u", "d")):
        its = {k: q_iters(k) for k in keys}
        seq = []
        for k in keys:
            for i in range(its[k][0]):
                seq.append((k, i))
        slots = [(qtmpA, b_wtmpA), (qtmpB, b_wtmpB)]
        started = {k: False for k in keys}
        k0, i0 = seq[0]
        dma(QQ[k0], QW[k0][5], slots[0][0][:, 0:its[k0][1]], q_src(k0, i0, 0),
            [], [slots[0][1]])
        for n, (k, i) in enumerate(seq):
            wt, bwt = slots[n % 2]
            n_it, w, nch, nh = its[k]
            if n + 1 < len(seq):
                k2, i2 = seq[n + 1]
                wt2, bwt2 = slots[(n + 1) % 2]
                dma(QQ[k2], QW[k2][5], wt2[:, 0:its[k2][1]], q_src(k2, i2, 0),
                    [], [bwt2])
            cx.op(qS, [bwt], [bwt], lambda e, wt=wt, w=w:
                  e.activation(wt[:, 0:w], wt[:, 0:w], AF.Abs))
            bank = QW[k][4]
            nsl = w // 512
            for sl in range(nsl):
                st = not started[k]
                started[k] = False if False else True
                last = (i == n_it - 1 and sl == nsl - 1)
                cx.op(qT, [bwt, b_consts], [b_pb[bank]],
                      lambda e, wt=wt, sl=sl, st=st, last=last, bank=bank:
                      e.matmul(pb[bank][0:1, 0:512], ones4[:, 0:1],
                               wt[:, sl * 512:(sl + 1) * 512],
                               start=st, stop=last))

    def quant_rows(key):
        bank = QW[key][4]
        nel = float(QW[key][2] * QW[key][3])
        cx.op(qV, [b_pb[bank]], [b_r11a], lambda e, bank=bank: e.tensor_reduce(
            r11a[:], pb[bank][0:1, 0:512], axis=AX, op=Alu.add))
        cx.op(qV, [b_r11a], [b_r11b], lambda e, nel=nel: e.tensor_scalar(
            r11b[:], r11a[:], 1.0 / nel, None, Alu.mult))
        cx.op(qV, [b_r11b], [b_r11a], lambda e: e.tensor_scalar(
            r11a[:], r11b[:], 1e-5, None, Alu.max))
        cx.op(qV, [b_r11a], [b_r11b], lambda e: e.reciprocal(r11b[:], r11a[:]))
        cx.op(qV, [b_r11b], [b_r11c], lambda e: e.reciprocal(r11c[:], r11b[:]))
        cx.op(qV, [b_r11c], [b_wc[key]], lambda e, t=wc_t[key]: e.tensor_scalar(
            t[:], r11c[:], 1.0 / 127.0, None, Alu.mult))
        cx.op(qT, [b_r11b, b_consts], [b_pb[7]], lambda e: e.matmul(
            pb[7][:, 0:1], ones1[:], r11b[:], start=True, stop=True))
        cx.op(qV, [b_pb[7]], [b_swp3[key]], lambda e, t=swp3[key]:
              e.tensor_copy(t[:], pb[7][:, 0:1]))

    def xs(c):
        return xbuf[:, c * T:(c + 1) * T]

    def xrs(c):
        return xrbuf[:, c * T:(c + 1) * T]

    def x_load(t):
        src = bass.AP(xT, t * T, [[L, 128], [L * 128, DC], [1, T]])
        dst = bass.AP(xbuf, 0, [[DC * T, 128], [T, DC], [1, T]])
        dma(qSY, dqSY, dst, src, [], [b_x])

    def out_store(t):
        dst = bass.AP(outT, t * T, [[L, 128], [L * 128, DC], [1, T]])
        src = bass.AP(c2buf, 0, [[DC * T, 128], [T, DC], [1, T]])
        dma(qSY, dqSY, dst, src, [b_c2], [b_out])

    def ln_stats(src_buf, src_tile):
        for c in range(DC):
            cx.op(qT, [src_buf, b_consts], [b_pb[4]],
                  lambda e, c=c: e.matmul(pb[4][0:4, :], ones4[:], src_tile(c),
                                          start=(c == 0), stop=(c == DC - 1)))
        for c in range(DC):
            sq = sqsc[:, (c % 2) * T:(c % 2) * T + T]
            cx.op(qV, [src_buf], [b_sq[c % 2]],
                  lambda e, c=c, sq=sq: e.tensor_tensor(
                      sq, src_tile(c), src_tile(c), Alu.mult))
            cx.op(qT, [b_sq[c % 2], b_consts], [b_pb[5]],
                  lambda e, c=c, sq=sq: e.matmul(
                      pb[5][0:4, :], ones4[:], sq,
                      start=(c == 0), stop=(c == DC - 1)))
        cx.op(qV, [b_pb[4]], [b_rm], lambda e: e.tensor_scalar(
            r_m[:], pb[4][0:1, :], 1.0 / D, None, Alu.mult))
        cx.op(qV, [b_pb[5]], [b_rt0], lambda e: e.tensor_scalar(
            r_t0[:], pb[5][0:1, :], 1.0 / D, None, Alu.mult))
        cx.op(qV, [b_rm], [b_rt1], lambda e: e.tensor_tensor(
            r_t1[:], r_m[:], r_m[:], Alu.mult))
        cx.op(qV, [b_rt0, b_rt1], [b_ram], lambda e: e.tensor_tensor(
            r_am[:], r_t0[:], r_t1[:], Alu.subtract))
        cx.op(qV, [b_ram], [b_rt1], lambda e: e.tensor_scalar(
            r_t1[:], r_am[:], EPS, None, Alu.add))
        cx.op(qS, [b_rt1], [b_rt0], lambda e: e.activation(
            r_t0[:], r_t1[:], AF.Sqrt))
        cx.op(qV, [b_rt0], [b_rs], lambda e: e.reciprocal(r_s[:], r_t0[:]))

    def bcast_row(row_h, row_buf, bank):
        def f(e):
            last = None
            for j in range(4):
                last = e.matmul(pb[bank][:, j * 128:(j + 1) * 128], ones1[:],
                                row_h[0:1, j * 128:(j + 1) * 128],
                                start=(j == 0), stop=(j == 3))
            return last
        cx.op(qT, [row_buf, b_consts], [b_pb[bank]], f)

    def amax_dance(mx_buf, mx_tile, width):
        nb = width // 128
        for j in range(nb):
            bk = 4 + (j % 2)
            cx.op(qT, [mx_buf, b_consts], [b_pb[bk]], lambda e, j=j, bk=bk:
                  e.transpose(pb[bk][:, 0:128],
                              mx_tile[:, j * 128:(j + 1) * 128], ident[:]))
            cx.op(qV, [b_pb[bk]], [b_atm], lambda e, j=j, bk=bk:
                  e.tensor_reduce(amax_tm[:, j:j + 1], pb[bk][:, 0:128],
                                  axis=AX, op=Alu.max,
                                  apply_absolute_value=True))
        cx.op(qT, [b_atm, b_consts], [b_pb[4]], lambda e: e.transpose(
            pb[4][0:4, 0:128], amax_tm[:], ident[:]))
        cx.op(qV, [b_pb[4]], [b_a4], lambda e: e.tensor_copy(
            a4[:], pb[4][0:4, 0:128]))
        dst = bass.AP(r_am, 0, [[T, 1], [1, width]])
        dma(qS, dqSC, dst, a4[0:nb, :], [b_a4], [b_ram])

    def wi_dma(g, q=None, dq=None):
        slot = g % 2
        for (dst_off, src_dram, bslab) in (
                (WIH[slot], win_hi, b_wih[slot]),
                (WIL[slot], win_lo, b_wil[slot])):
            for (blk, coloff) in ((0, g * 128), (1, INNER + g * 128)):
                src = bass.AP(src_dram, coloff,
                              [[F, 128], [F * 128, DC], [1, 128]])
                dst = bass.AP(arena, dst_off + blk * 128,
                              [[16384, 128], [256, DC], [1, 128]])
                dma(q or qSY, dq or dqSY, dst, src, [b_dwin], [bslab])

    def wo_dma(sl, q=None, dq=None):
        slot = sl % 2
        for (dst_off, src_dram, bslab) in (
                (WOH[slot], wout_hi, b_woh[slot]),
                (WOL[slot], wout_lo, b_wol[slot])):
            src = bass.AP(src_dram, sl * 2 * 128 * D,
                          [[D, 128], [D * 128, 2], [1, D]])
            dst = bass.AP(arena, dst_off, [[16384, 128], [1024, 2], [1, D]])
            dma(q or qSY, dq or dqSY, dst, src, [b_dwout], [bslab])

    def wi_split(g):
        slot = g % 2
        r = g % 4
        bwt = b_stg4[r]
        for (blk, coloff) in ((0, g * 128), (1, INNER + g * 128)):
            srcap = bass.AP(w_in, coloff, [[F, 128], [F * 128, DC], [1, 128]])
            dst = bass.AP(p_buf, r * 2048 + blk * 128,
                          [[8192, 128], [256, DC], [1, 128]])
            dma(qSY, dqSY, dst, srcap, [], [bwt])
        hi_ap = arena[:, WIH[slot]:WIH[slot] + 2048]
        lo_ap = arena[:, WIL[slot]:WIL[slot] + 2048]
        src_ap = p_buf[:, r * 2048:r * 2048 + 2048]
        cx.op(qS, [bwt], [b_wih[slot]], lambda e, hi_ap=hi_ap, src_ap=src_ap:
              e.copy(hi_ap, src_ap))
        cx.op(qV, [bwt, b_wih[slot]], [b_wil[slot]],
              lambda e, hi_ap=hi_ap, lo_ap=lo_ap, src_ap=src_ap:
              e.tensor_tensor(lo_ap, src_ap, hi_ap, Alu.subtract))
        for (blk, coloff) in ((0, g * 128), (1, INNER + g * 128)):
            dhi = bass.AP(win_hi, coloff, [[F, 128], [F * 128, DC], [1, 128]])
            dlo = bass.AP(win_lo, coloff, [[F, 128], [F * 128, DC], [1, 128]])
            shi = bass.AP(arena, WIH[slot] + blk * 128,
                          [[16384, 128], [256, DC], [1, 128]])
            slo = bass.AP(arena, WIL[slot] + blk * 128,
                          [[16384, 128], [256, DC], [1, 128]])
            dma(qS, dqSC, dhi, shi, [b_wih[slot]], [b_dwin])
            dma(qS, dqSC, dlo, slo, [b_wil[slot]], [b_dwin])

    def wo_split(sl):
        slot = sl % 2
        r = 2 + sl % 2
        bwt = b_stg4[r]
        srcap = bass.AP(w_out, sl * 2 * 128 * D, [[D, 128], [D * 128, 2], [1, D]])
        dst = bass.AP(p_buf, r * 2048, [[8192, 128], [1024, 2], [1, D]])
        dma(qSY, dqSY, dst, srcap, [], [bwt])
        hi_ap = arena[:, WOH[slot]:WOH[slot] + 2048]
        lo_ap = arena[:, WOL[slot]:WOL[slot] + 2048]
        src_ap = p_buf[:, r * 2048:r * 2048 + 2048]
        cx.op(qS, [bwt], [b_woh[slot]], lambda e, hi_ap=hi_ap, src_ap=src_ap:
              e.copy(hi_ap, src_ap))
        cx.op(qV, [bwt, b_woh[slot]], [b_wol[slot]],
              lambda e, hi_ap=hi_ap, lo_ap=lo_ap, src_ap=src_ap:
              e.tensor_tensor(lo_ap, src_ap, hi_ap, Alu.subtract))
        dhi = bass.AP(wout_hi, sl * 2 * 128 * D, [[D, 128], [D * 128, 2], [1, D]])
        dlo = bass.AP(wout_lo, sl * 2 * 128 * D, [[D, 128], [D * 128, 2], [1, D]])
        shi = bass.AP(arena, WOH[slot], [[16384, 128], [1024, 2], [1, D]])
        slo = bass.AP(arena, WOL[slot], [[16384, 128], [1024, 2], [1, D]])
        dma(qS, dqSC, dhi, shi, [b_woh[slot]], [b_dwout])
        dma(qS, dqSC, dlo, slo, [b_wol[slot]], [b_dwout])

    def wg_dma(m):
        slot = m % 2
        for (off, dram_t, bslab, bd) in (
                (WIH[slot], wgq, b_wih[slot], b_dram_q["g"]),
                (WIL[slot], wuq, b_wil[slot], b_dram_q["u"])):
            src = bass.AP(dram_t, 2 * m * 128,
                          [[FF, 128], [FF * 128, DC], [1, 256]])
            dst = bass.AP(arena, off, [[16384, 128], [256, DC], [1, 256]])
            dma(qG, dqG, dst, src, [bd], [bslab])

    def wg_quant_slab(m):
        for (key, src_dram, dst_dram, off, bslab, b_stg, stg_off) in (
                ("g", wgT, wgq, WIH[m % 2], b_wih[m % 2], b_stg_g, 0),
                ("u", wuT, wuq, WIL[m % 2], b_wil[m % 2], b_stg_u, 2048)):
            srcap = bass.AP(src_dram, 2 * m * 128,
                            [[FF, 128], [FF * 128, DC], [1, 256]])
            dst_in = bass.AP(c2buf, stg_off, [[4096, 128], [256, DC], [1, 256]])
            dma(qS, dqSC, dst_in, srcap, [], [b_stg])
            in_ap = c2buf[:, stg_off:stg_off + 2048]
            qi_ap = c2I[:, stg_off:stg_off + 2048]
            slot_ap = arena[:, off:off + 2048]
            cx.op(qS, [b_swp3[key]], [b_stg],
                  lambda e, in_ap=in_ap, qi_ap=qi_ap, key=key:
                  e.activation(qi_ap, in_ap, AF.Copy, scale=swp3[key][:]))
            cx.op(qV, [b_stg], [bslab],
                  lambda e, qi_ap=qi_ap, slot_ap=slot_ap:
                  e.tensor_scalar(slot_ap, qi_ap, -1.0, 1.0, Alu.max, Alu.min))
            dq_ap = bass.AP(dst_dram, 2 * m * 128,
                            [[FF, 128], [FF * 128, DC], [1, 256]])
            sl_ap = bass.AP(arena, off, [[16384, 128], [256, DC], [1, 256]])
            dma(qSY, dqSY, dq_ap, sl_ap, [bslab], [b_dram_q[key]])

    def wd_quant_slab(m):
        b_stg = b_stg_d[m % 2]
        stgF = xnlF if m % 2 == 0 else xnhF
        stgI = xnlI if m % 2 == 0 else xnhI
        slot = m % 2
        srcap = bass.AP(wdT, 2 * m * 128 * D, [[D, 128], [D * 128, 2], [1, D]])
        dst_in = bass.AP(stgF, 0, [[2048, 128], [1024, 2], [1, D]])
        dma(qS, dqSC, dst_in, srcap, [], [b_stg])
        slot_ap = arena[:, WOH[slot]:WOH[slot] + 2048]
        cx.op(qS, [b_swp3["d"]], [b_stg],
              lambda e, stgF=stgF, stgI=stgI:
              e.activation(stgI[:, 0:2048], stgF[:, 0:2048], AF.Copy,
                           scale=swp3["d"][:]))
        cx.op(qV, [b_stg], [b_woh[slot]],
              lambda e, stgI=stgI, slot_ap=slot_ap:
              e.tensor_scalar(slot_ap, stgI[:, 0:2048], -1.0, 1.0,
                              Alu.max, Alu.min))
        dq_ap = bass.AP(wdq, 2 * m * 128 * D, [[D, 128], [D * 128, 2], [1, D]])
        sl_ap = bass.AP(arena, WOH[slot], [[16384, 128], [1024, 2], [1, D]])
        dma(qSY, dqSY, dq_ap, sl_ap, [b_woh[slot]], [b_dram_q["d"], b_dwd[m]])

    def wd_dma(m):
        slot = m % 2
        src = bass.AP(wdq, 2 * m * 128 * D, [[D, 128], [D * 128, 2], [1, D]])
        dst = bass.AP(arena, WOH[slot], [[16384, 128], [1024, 2], [1, D]])
        dma(qG, dqG, dst, src, [b_dwd[m]], [b_woh[slot]])

    def p_ap(o):
        if o < 16:
            return p_buf[:, o * T:(o + 1) * T]
        return ybig32[:, (o - 16) * T:(o - 15) * T]

    def p_bufs(o):
        return b_p if o < 16 else b_ybig

    def ffn(t, pre_next=False):
        wg_fn = wg_quant_slab if t == 0 else wg_dma
        wd_fn = wd_quant_slab if t == 0 else wd_dma
        gu_units, dn_units = ([], [])
        if pre_next:
            gu_units, dn_units = pre_hook_units()
        if t == 0:
            for bb in (b_stg_g, b_stg_u):
                bb.r.append((qV, qV.count))
            b_stg_d[0].r.append((qT, qT.count))
        wg_fn(0)
        wg_fn(1)
        mxs = [sqsc[:, 0:T], sqsc[:, T:2 * T]]
        cur = 0
        for o in range(NO):
            slot = (o // 2) % 2
            wb = 3 if t == 0 else 2
            gb, ub = 2 * (o % wb), 2 * (o % wb) + 1

            if o < 2:
                for c in range(DC):
                    def mm_gu1(e, o=o, slot=slot, gb=gb, ub=ub, c=c):
                        j = o % 2
                        lg = arena[:, WIH[slot] + c * 256 + j * 128:
                                   WIH[slot] + c * 256 + j * 128 + 128]
                        lu = arena[:, WIL[slot] + c * 256 + j * 128:
                                   WIL[slot] + c * 256 + j * 128 + 128]
                        rq = xnh[:, c * T:(c + 1) * T]
                        e.matmul(pb[gb][:], lg, rq,
                                 start=(c == 0), stop=(c == DC - 1))
                        return e.matmul(pb[ub][:], lu, rq,
                                        start=(c == 0), stop=(c == DC - 1))
                    cx.op(qT, [b_wih[slot], b_wil[slot], b_xq[c]],
                          [b_pb[gb], b_pb[ub]], mm_gu1)
            else:
                def mm_gu(e, o=o, slot=slot, gb=gb, ub=ub):
                    last = None
                    j = o % 2
                    for c in range(DC):
                        lg = arena[:, WIH[slot] + c * 256 + j * 128:
                                   WIH[slot] + c * 256 + j * 128 + 128]
                        lu = arena[:, WIL[slot] + c * 256 + j * 128:
                                   WIL[slot] + c * 256 + j * 128 + 128]
                        rq = xnh[:, c * T:(c + 1) * T]
                        e.matmul(pb[gb][:], lg, rq,
                                 start=(c == 0), stop=(c == DC - 1))
                        last = e.matmul(pb[ub][:], lu, rq,
                                        start=(c == 0), stop=(c == DC - 1))
                    return last
                cx.op(qT, [b_wih[slot], b_wil[slot]] + [b_xq[c] for c in range(DC)],
                      [b_pb[gb], b_pb[ub]], mm_gu)
            if o % 2 == 1 and o // 2 + 2 < NO // 2:
                wg_fn(o // 2 + 2)
            if gu_units and o % 2 == 0 and o >= 2:
                gu_units.pop(0)()
            cx.op(qV, [b_pb[gb], b_rg], [b_sg2],
                  lambda e, gb=gb: e.tensor_tensor(
                      sg2[:], pb[gb][:], rgsb[:], Alu.mult))
            cx.op(qS, [b_sg2], [b_sgs], lambda e:
                  e.activation(sgs[:], sg2[:], AF.Sigmoid))
            cx.op(qV, [b_sg2, b_sgs], [b_dsc], lambda e:
                  e.tensor_tensor(dsc[:], sg2[:], sgs[:], Alu.mult))
            cx.op(qV, [b_dsc, b_pb[ub]], [p_bufs(o)],
                  lambda e, o=o, ub=ub: e.tensor_tensor(
                      p_ap(o), dsc[:], pb[ub][:], Alu.mult))
            if o == 0:
                cx.op(qV, [p_bufs(o)], [b_sq[0]],
                      lambda e, o=o: e.scalar_tensor_tensor(
                          mxs[0], p_ap(o), -1.0, p_ap(o), Alu.mult, Alu.max))
                cur = 0
            else:
                cx.op(qV, [p_bufs(o), b_sq[cur]], [b_sq[1 - cur]],
                      lambda e, o=o, cur=cur: e.scalar_tensor_tensor(
                          mxs[1 - cur], p_ap(o), -1.0, mxs[cur],
                          Alu.mult, Alu.max))
                cur = 1 - cur
                cx.op(qV, [p_bufs(o), b_sq[cur]], [b_sq[1 - cur]],
                      lambda e, o=o, cur=cur: e.tensor_tensor(
                          mxs[1 - cur], p_ap(o), mxs[cur], Alu.max))
                cur = 1 - cur
        if t == 0:
            b_stg_d[1].r.append((qT, qT.count))
            wd_fn(0)
        amax_dance(b_sq[cur], mxs[cur], T)
        cx.op(qV, [b_ram, b_rru], [b_rt0], lambda e: e.tensor_tensor(
            r_t0[:], r_am[:], r_ru[:], Alu.mult))
        cx.op(qV, [b_rt0], [b_rt1], lambda e: e.tensor_scalar(
            r_t1[:], r_t0[:], 1e-5, None, Alu.max))
        cx.op(qV, [b_rt1], [b_rt0], lambda e: e.reciprocal(
            r_t0[:], r_t1[:]))
        cx.op(qV, [b_rt0], [b_ram], lambda e: e.tensor_scalar(
            r_am[:], r_t0[:], 127.0, None, Alu.mult))
        cx.op(qV, [b_ram, b_rru], [b_rt0], lambda e: e.tensor_tensor(
            r_t0[:], r_am[:], r_ru[:], Alu.mult))
        cx.op(qV, [b_rt1, b_wc["d"]], [b_ram], lambda e: e.tensor_scalar(
            r_am[:], r_t1[:], wc_d[:], None, Alu.mult))

        def f_bc(e):
            last = None
            for j in range(4):
                e.matmul(pb[6][:, j * 128:(j + 1) * 128], ones1[:],
                         r_t0[0:1, j * 128:(j + 1) * 128],
                         start=(j == 0), stop=(j == 3))
            for j in range(4):
                last = e.matmul(pb[7][:, j * 128:(j + 1) * 128],
                                ones1[:], r_am[0:1, j * 128:(j + 1) * 128],
                                start=(j == 0), stop=(j == 3))
            return last
        cx.op(qT, [b_rt0, b_ram, b_consts], [b_pb[6], b_pb[7]], f_bc)
        cx.op(qV, [b_pb[6]], [b_bsb], lambda e: e.tensor_copy(
            bsb[:], pb[6][:]))
        cx.op(qV, [b_pb[7]], [b_rd], lambda e: e.tensor_copy(
            rdsb[:], pb[7][:]))
        if pre_next:
            wi_dma(0)
            wi_dma(1)
        if t != 0:
            wd_fn(0)
        wd_fn(1)
        for o in range(NO):
            h = o % 2
            slot = (o // 2) % 2
            cx.op(qV, [p_bufs(o), b_bsb], [b_qi[0]], lambda e, o=o:
                  e.tensor_tensor(qisc[:], p_ap(o), bsb[:], Alu.mult))
            cx.op(qV, [b_qi[0]], [b_qr[h]], lambda e, h=h: e.tensor_scalar(
                qring[:, h * T:(h + 1) * T], qisc[:],
                -128.0, 127.0, Alu.max, Alu.min))

            def mm_d(e, o=o, h=h, slot=slot):
                last = None
                j = o % 2
                for d in range(8):
                    lw = arena[:, WOH[slot] + j * 1024 + d * 128:
                               WOH[slot] + j * 1024 + d * 128 + 128]
                    last = e.matmul(pb[d][:], lw,
                                    qring[:, h * T:(h + 1) * T],
                                    start=(o == 0), stop=(o == NO - 1))
                return last
            cx.op(qT, [b_woh[slot], b_qr[h]],
                  [b_pb[d] for d in range(8)], mm_d)
            if o % 2 == 1 and o // 2 + 2 < NO // 2:
                wd_fn(o // 2 + 2)
            if dn_units and o % 4 == 0:
                dn_units.pop(0)()
        if t == 0:
            b_c2.r.append((qS, qS.count))
            for bb in (b_xnh, b_xnl):
                bb.r.append((qS, qS.count))
                bb.r.append((qV, qV.count))
        for d in range(DC):
            cx.op(qV, [b_pb[d], b_rd], [b_dsc], lambda e, d=d:
                  e.tensor_tensor(dsc[:], pb[d][:], rdsb[:], Alu.mult))
            cx.op(qV, [b_dsc, b_xr], [b_c2],
                  lambda e, d=d: e.tensor_tensor(
                      c2buf[:, d * T:(d + 1) * T], dsc[:],
                      xrs(d), Alu.add))

    def tile_mixer_pre(t):
        ln_stats(b_x, xs)
        bcast_row(r_m, b_rm, 6)
        cx.op(qV, [b_pb[6]], [b_m1], lambda e: e.tensor_copy(m1sb[:], pb[6][:]))
        for c in range(DC):
            cx.op(qV, [b_x, b_m1], [b_c2], lambda e, c=c: e.tensor_tensor(
                c2buf[:, c * T:(c + 1) * T], xs(c), m1sb[:], Alu.subtract))
        bcast_row(r_s, b_rs, 7)
        cx.op(qV, [b_pb[7]], [b_s1], lambda e: e.tensor_copy(s1sb[:], pb[7][:]))
        xn2v = sqsc[:, 0:T]
        for c in range(DC):
            cx.op(qV, [b_c2, b_s1], [b_sq[0]], lambda e, c=c, xn2v=xn2v:
                  e.tensor_tensor(xn2v, c2buf[:, c * T:(c + 1) * T],
                                  s1sb[:], Alu.mult))
            cx.op(qV, [b_sq[0]], [b_xnh, b_xq[c]], lambda e, c=c, xn2v=xn2v:
                  e.tensor_copy(xnh[:, c * T:(c + 1) * T], xn2v))
            cx.op(qV, [b_sq[0], b_xnh], [b_xnl], lambda e, c=c, xn2v=xn2v:
                  e.tensor_tensor(xnl[:, c * T:(c + 1) * T], xn2v,
                                  xnh[:, c * T:(c + 1) * T], Alu.subtract))


    def pre_hook_units():
        gu = []

        def u_sums():
            def f(e):
                last = None
                for c in range(DC):
                    last = e.matmul(pb[4][0:4, :], ones4[:], xs(c),
                                    start=(c == 0), stop=(c == DC - 1))
                return last
            cx.op(qT, [b_x, b_consts], [b_pb[4]], f)
        gu.append(u_sums)

        def u_sumsq(c):
            def emit():
                cx.op(qV, [b_x], [b_xnsc], lambda e, c=c: e.tensor_tensor(
                    xnsc[:], xs(c), xs(c), Alu.mult))
                cx.op(qT, [b_xnsc, b_consts], [b_pb[5]],
                      lambda e, c=c: e.matmul(pb[5][0:4, :], ones4[:],
                                              xnsc[:], start=(c == 0),
                                              stop=(c == DC - 1)))
            return emit
        for c in range(DC):
            gu.append(u_sumsq(c))

        def u_rows():
            cx.op(qV, [b_pb[4]], [b_rm], lambda e: e.tensor_scalar(
                r_m[:], pb[4][0:1, :], 1.0 / D, None, Alu.mult))
            cx.op(qV, [b_pb[5]], [b_rt0], lambda e: e.tensor_scalar(
                r_t0[:], pb[5][0:1, :], 1.0 / D, None, Alu.mult))
            cx.op(qV, [b_rm], [b_rt1], lambda e: e.tensor_tensor(
                r_t1[:], r_m[:], r_m[:], Alu.mult))
            cx.op(qV, [b_rt0, b_rt1], [b_ram], lambda e: e.tensor_tensor(
                r_am[:], r_t0[:], r_t1[:], Alu.subtract))
            cx.op(qV, [b_ram], [b_rt1], lambda e: e.tensor_scalar(
                r_t1[:], r_am[:], EPS, None, Alu.add))
            cx.op(qS, [b_rt1], [b_rt0], lambda e: e.activation(
                r_t0[:], r_t1[:], AF.Sqrt))
            cx.op(qV, [b_rt0], [b_rs], lambda e: e.reciprocal(r_s[:], r_t0[:]))
        gu.append(u_rows)

        def u_bc_m():
            bcast_row(r_m, b_rm, 6)
            cx.op(qV, [b_pb[6]], [b_m1],
                  lambda e: e.tensor_copy(m1sb[:], pb[6][:]))
        gu.append(u_bc_m)

        def u_bc_s():
            bcast_row(r_s, b_rs, 7)
            cx.op(qV, [b_pb[7]], [b_s1],
                  lambda e: e.tensor_copy(s1sb[:], pb[7][:]))
        gu.append(u_bc_s)

        dn = []

        def u_xn(c):
            def emit():
                xn2v = sqsc[:, 0:T]
                cx.op(qV, [b_x, b_m1], [b_c2], lambda e, c=c: e.tensor_tensor(
                    c2buf[:, c * T:(c + 1) * T], xs(c), m1sb[:],
                    Alu.subtract))
                cx.op(qV, [b_c2, b_s1], [b_sq[0]], lambda e, c=c, xn2v=xn2v:
                      e.tensor_tensor(xn2v, c2buf[:, c * T:(c + 1) * T],
                                      s1sb[:], Alu.mult))
                cx.op(qV, [b_sq[0]], [b_xnh, b_xq[c]],
                      lambda e, c=c, xn2v=xn2v: e.tensor_copy(
                          xnh[:, c * T:(c + 1) * T], xn2v))
                cx.op(qV, [b_sq[0], b_xnh], [b_xnl],
                      lambda e, c=c, xn2v=xn2v: e.tensor_tensor(
                          xnl[:, c * T:(c + 1) * T], xn2v,
                          xnh[:, c * T:(c + 1) * T], Alu.subtract))
            return emit
        for c in range(DC):
            dn.append(u_xn(c))
        return gu, dn

    def tile_mixer_main(t, skip_slabs=False):
        wi_fn = wi_split if t == 0 else wi_dma
        wo_fn = wo_split if t == 0 else wo_dma
        if not skip_slabs:
            wi_fn(0)
            wi_fn(1)
        for g in range(NA):
            slot = g % 2
            ba, bg = 2 * (g % 4), 2 * (g % 4) + 1

            def mm_u(e, slot=slot, ba=ba, bg=bg):
                last = None
                for c in range(DC):
                    for (bank, blk) in ((ba, 0), (bg, 1)):
                        lh = arena[:, WIH[slot] + c * 256 + blk * 128:
                                   WIH[slot] + c * 256 + blk * 128 + 128]
                        ll = arena[:, WIL[slot] + c * 256 + blk * 128:
                                   WIL[slot] + c * 256 + blk * 128 + 128]
                        rh = xnh[:, c * T:(c + 1) * T]
                        rl = xnl[:, c * T:(c + 1) * T]
                        e.matmul(pb[bank][:], lh, rh, start=(c == 0), stop=False)
                        e.matmul(pb[bank][:], lh, rl, start=False, stop=False)
                        last = e.matmul(pb[bank][:], ll, rh, start=False,
                                        stop=(c == DC - 1))
                return last
            cx.op(qT, [b_wih[slot], b_wil[slot], b_xnh, b_xnl],
                  [b_pb[ba], b_pb[bg]], mm_u)
            sg_half = sigb[:, 0:T]
            cx.op(qS, [b_pb[bg]], [b_sig],
                  lambda e, bg=bg, sg_half=sg_half:
                  e.activation(sg_half, pb[bg][:], AF.Sigmoid))
            cx.op(qV, [b_carry[g]], [b_glu], lambda e, g=g: e.tensor_copy(
                glu[:, 0:3], carry[:, 3 * g:3 * g + 3]))
            cx.op(qV, [b_pb[ba], b_sig, b_glu], [b_glu],
                  lambda e, ba=ba, sg_half=sg_half: e.tensor_tensor(
                      glu[:, 3:3 + T], pb[ba][:], sg_half, Alu.mult))
            cx.op(qV, [b_glu], [b_carry[g]], lambda e, g=g: e.tensor_copy(
                carry[:, 3 * g:3 * g + 3], glu[:, T:T + 3]))
            ya, yb = yact[:, 0:T], yact[:, T:2 * T]
            cx.op(qV, [b_glu, b_consts], [b_yact[0]], lambda e, g=g, ya=ya:
                  e.tensor_scalar(ya, glu[:, 0:T],
                                  convw[:, 4 * g:4 * g + 1],
                                  convb[:, g:g + 1], Alu.mult, Alu.add))
            cur_t, oth_t = yb, ya
            buf_c, buf_o = b_yact[1], b_yact[0]
            for k in range(1, 4):
                cx.op(qV, [b_glu, b_consts, buf_o], [buf_c],
                      lambda e, g=g, k=k, ct=cur_t, ot=oth_t:
                      e.scalar_tensor_tensor(
                          ct, glu[:, k:k + T],
                          convw[:, 4 * g + k:4 * g + k + 1],
                          ot, Alu.mult, Alu.add))
                cur_t, oth_t = oth_t, cur_t
                buf_c, buf_o = buf_o, buf_c
            cx.op(qS, [buf_o], [b_sig], lambda e, rt=oth_t:
                  e.activation(sigb[:, 0:T], rt, AF.Sigmoid))
            cx.op(qV, [buf_o, b_sig], [buf_c],
                  lambda e, rt=oth_t, ot2=cur_t:
                  e.tensor_tensor(ot2, rt, sigb[:, 0:T], Alu.mult))
            cx.op(qV, [buf_c], [b_ybig], lambda e, g=g, ot2=cur_t: e.tensor_copy(
                yh[:, g * T:(g + 1) * T], ot2))
            cx.op(qV, [buf_c, b_ybig], [b_ybig], lambda e, g=g, ot2=cur_t:
                  e.tensor_tensor(yl[:, g * T:(g + 1) * T], ot2,
                                  yh[:, g * T:(g + 1) * T], Alu.subtract))
            if g + 2 < NA:
                wi_fn(g + 2)
            elif g == NA - 2:
                wo_fn(0)
                wo_fn(1)
        if t == 0:
            for bb in (b_wtmpA, b_wtmpB):
                bb.r.append((qS, qS.count))
                bb.r.append((qV, qV.count))

        for f_ in range(NA):
            slot = (f_ // 2) % 2

            def mm_o(e, f_=f_, slot=slot):
                last = None
                j = f_ % 2
                for d in range(DC):
                    lh = arena[:, WOH[slot] + j * 1024 + d * 128:
                               WOH[slot] + j * 1024 + d * 128 + 128]
                    ll = arena[:, WOL[slot] + j * 1024 + d * 128:
                               WOL[slot] + j * 1024 + d * 128 + 128]
                    rh = yh[:, f_ * T:(f_ + 1) * T]
                    rl = yl[:, f_ * T:(f_ + 1) * T]
                    e.matmul(pb[d][:], lh, rh, start=(f_ == 0), stop=False)
                    e.matmul(pb[d][:], lh, rl, start=False, stop=False)
                    last = e.matmul(pb[d][:], ll, rh, start=False,
                                    stop=(f_ == NA - 1))
                return last
            cx.op(qT, [b_woh[slot], b_wol[slot], b_ybig],
                  [b_pb[d] for d in range(8)], mm_o)
            if f_ % 2 == 1 and f_ // 2 + 2 < 8:
                wo_fn(f_ // 2 + 2)
        for d in range(DC):
            cx.op(qV, [b_pb[d], b_x], [b_xr], lambda e, d=d: e.tensor_tensor(
                xrs(d), pb[d][:], xs(d), Alu.add))
        if t + 1 < n_tiles:
            x_load(t + 1)


    def tile_ffn(t, pre_next=False):
        ln_stats(b_xr, xrs)
        bcast_row(r_m, b_rm, 6)
        cx.op(qV, [b_pb[6]], [b_m1], lambda e: e.tensor_copy(m1sb[:], pb[6][:]))
        mxs = [sqsc[:, 0:T], sqsc[:, T:2 * T]]
        cur = 0
        for c in range(DC):
            cx.op(qV, [b_xr, b_m1], [b_c2], lambda e, c=c: e.tensor_tensor(
                c2buf[:, c * T:(c + 1) * T], xrs(c), m1sb[:], Alu.subtract))
            if c == 0:
                cx.op(qV, [b_c2], [b_sq[0]], lambda e: e.scalar_tensor_tensor(
                    mxs[0], c2buf[:, 0:T], -1.0, c2buf[:, 0:T],
                    Alu.mult, Alu.max))
            else:
                cx.op(qV, [b_c2, b_sq[cur]], [b_sq[1 - cur]],
                      lambda e, c=c, cur=cur: e.scalar_tensor_tensor(
                          mxs[1 - cur], c2buf[:, c * T:(c + 1) * T], -1.0,
                          mxs[cur], Alu.mult, Alu.max))
                cur = 1 - cur
                cx.op(qV, [b_c2, b_sq[cur]], [b_sq[1 - cur]],
                      lambda e, c=c, cur=cur: e.tensor_tensor(
                          mxs[1 - cur], c2buf[:, c * T:(c + 1) * T],
                          mxs[cur], Alu.max))
                cur = 1 - cur
        amax_dance(b_sq[cur], mxs[cur], T)
        cx.op(qV, [b_ram, b_rs], [b_rt0], lambda e: e.tensor_tensor(
            r_t0[:], r_am[:], r_s[:], Alu.mult))
        cx.op(qV, [b_rt0], [b_rt1], lambda e: e.tensor_scalar(
            r_t1[:], r_t0[:], 1e-5, None, Alu.max))
        cx.op(qV, [b_rt1], [b_rt0], lambda e: e.reciprocal(r_t0[:], r_t1[:]))
        cx.op(qV, [b_rt0], [b_ram], lambda e: e.tensor_scalar(
            r_am[:], r_t0[:], 127.0, None, Alu.mult))
        cx.op(qV, [b_ram, b_rs], [b_rt0], lambda e: e.tensor_tensor(
            r_t0[:], r_am[:], r_s[:], Alu.mult))
        bcast_row(r_t0, b_rt0, 7)
        cx.op(qV, [b_pb[7]], [b_a2], lambda e: e.tensor_copy(a2sb[:], pb[7][:]))
        cx.op(qV, [b_rt1, b_wc["g"]], [b_ram], lambda e: e.tensor_scalar(
            r_am[:], r_t1[:], wc_g[:], None, Alu.mult))
        cx.op(qV, [b_rt1, b_wc["u"]], [b_rru], lambda e: e.tensor_scalar(
            r_ru[:], r_t1[:], wc_u[:], None, Alu.mult))
        bcast_row(r_am, b_ram, 6)
        cx.op(qV, [b_pb[6]], [b_rg], lambda e: e.tensor_copy(rgsb[:], pb[6][:]))
        for c in range(DC):
            cx.op(qV, [b_c2, b_a2], [b_qi[0]], lambda e, c=c:
                  e.tensor_tensor(qisc[:], c2buf[:, c * T:(c + 1) * T],
                                  a2sb[:], Alu.mult))
            cx.op(qV, [b_qi[0], b_xnh], [b_xq[c]], lambda e, c=c:
                  e.tensor_scalar(xnh[:, c * T:(c + 1) * T], qisc[:],
                                  -128.0, 127.0, Alu.max, Alu.min))

        ffn(t, pre_next)

        out_store(t)

    x_load(0)
    tile_mixer_pre(0)

    for t in range(n_tiles):
        if t == 0:
            tile_mixer_main(0)
            quant_pass1(("g",))
            quant_rows("g")
            quant_pass1(("u",))
            quant_rows("u")
            quant_pass1(("d",))
            quant_rows("d")
            f2 = [(qV, qV.count), (qT, qT.count), (qS, qS.count)]
            for sq_ in list(dqG) + list(dqSC):
                if sq_.count:
                    f2.append((sq_, sq_.count))
            b_p.r.extend(f2)
            tile_ffn(0)
        else:
            if t == 1:
                tile_mixer_pre(t)
            tile_mixer_main(t, skip_slabs=(t >= 2))
            tile_ffn(t, pre_next=(t < n_tiles - 1))

    for i, sq_ in enumerate(dqSY):
        if sq_.count:
            thSY.append(lambda e, s=sq_.sem, c=sq_.count: e.wait_ge(s, c))
    for i, sq_ in enumerate(dqG):
        if sq_.count:
            thG.append(lambda e, s=sq_.sem, c=sq_.count: e.wait_ge(s, c))
    for i, sq_ in enumerate(dqSC):
        if sq_.count:
            thS.append(lambda e, s=sq_.sem, c=sq_.count: e.wait_ge(s, c))

    with nc.Block() as block:
        @block.tensor
        def _(eng):
            for th in thT:
                th(eng)

        @block.vector
        def _(eng):
            for th in thV:
                th(eng)

        @block.scalar
        def _(eng):
            for th in thS:
                th(eng)

        @block.gpsimd
        def _(eng):
            for th in thG:
                th(eng)

        @block.sync
        def _(eng):
            for th in thSY:
                th(eng)

    return nc


def kernel(L0_attn_x, L0_attn_W_in, L0_attn_conv_w, L0_attn_conv_b,
           L0_attn_W_out, L0_ffn_w_gate, L0_ffn_w_up, L0_ffn_w_down):
    x = np.asarray(L0_attn_x)
    W_in = np.ascontiguousarray(np.asarray(L0_attn_W_in))
    W_out = np.ascontiguousarray(np.asarray(L0_attn_W_out))
    conv_w = np.asarray(L0_attn_conv_w)
    conv_b = np.asarray(L0_attn_conv_b)
    wgT_np = np.ascontiguousarray(np.asarray(L0_ffn_w_gate).T)
    wuT_np = np.ascontiguousarray(np.asarray(L0_ffn_w_up).T)
    wdT_np = np.ascontiguousarray(np.asarray(L0_ffn_w_down).T)
    convw_np = np.ascontiguousarray(
        conv_w.reshape(16, 128, 4).transpose(1, 0, 2).reshape(128, 64))
    convb_np = np.ascontiguousarray(conv_b.reshape(16, 128).T)

    nc = build_program(n_tiles=8)
    in_maps = []
    for b in range(8):
        xT_b = np.ascontiguousarray(x[b].T)
        in_maps.append({
            "xT": xT_b, "w_in": W_in, "w_out": W_out,
            "convw": convw_np, "convb": convb_np,
            "wgT": wgT_np, "wuT": wuT_np, "wdT": wdT_np,
        })
    res = run_bass_kernel_spmd(nc, in_maps, core_ids=list(range(8)))
    out = np.stack([np.asarray(res.results[b]["outT"]).T for b in range(8)],
                   axis=0)
    return np.ascontiguousarray(out, dtype=np.float32)
